# version 01
# baseline (speedup 1.0000x reference)
import contextlib
import numpy as np
import concourse.bass as bass
import concourse.mybir as mybir
from concourse.bass_utils import run_bass_kernel_spmd

F32 = mybir.dt.float32
BF16 = mybir.dt.bfloat16
AF = mybir.ActivationFunctionType
ALU = mybir.AluOpType
AX = mybir.AxisListType

CLAMP = 23.02585093
B = 2
S = 2048
D = 2048
TIL = 128
NT = S // TIL
GT = 8
NG = NT // GT
GTOK = GT * TIL
NK = D // 128
FF = 8192
NF = FF // 128
NCH = GTOK // 64


class Sem:
    def __init__(self, stack, nc, name):
        self.h = stack.enter_context(nc.semaphore(name))
        self.n = 0
        self.name = name


class Stream:
    def __init__(self, name):
        self.name = name
        self.ops = []

    def wait(self, sem, val):
        if val > 0:
            self.ops.append(("w", sem, val))

    def emit(self, fn, inc=None, inc_n=1):
        self.ops.append(("o", fn, inc, inc_n))
        if inc is not None:
            inc.n += inc_n

    def play(self, eng):
        for rec in self.ops:
            if rec[0] == "w":
                eng.wait_ge(rec[1].h, rec[2])
            else:
                inst = rec[1](eng)
                if rec[2] is not None:
                    inst.then_inc(rec[2].h, rec[3])


def check_deadlock(streams):
    cnt = {}
    idx = {st.name: 0 for st in streams}
    progress = True
    while progress:
        progress = False
        for st in streams:
            while idx[st.name] < len(st.ops):
                rec = st.ops[idx[st.name]]
                if rec[0] == "w":
                    if cnt.get(rec[1].name, 0) < rec[2]:
                        break
                else:
                    if rec[2] is not None:
                        cnt[rec[2].name] = cnt.get(rec[2].name, 0) + rec[3]
                idx[st.name] += 1
                progress = True
    stuck = {st.name: (idx[st.name], len(st.ops)) for st in streams
             if idx[st.name] < len(st.ops)}
    if stuck:
        details = []
        for st in streams:
            if idx[st.name] < len(st.ops):
                rec = st.ops[idx[st.name]]
                details.append(f"{st.name}@{idx[st.name]}/{len(st.ops)} waiting "
                               f"{rec[1].name}>={rec[2]} (have {cnt.get(rec[1].name, 0)})")
        raise RuntimeError("DEADLOCK: " + "; ".join(details))


def build(dbg=False):
    nc = bass.Bass("TRN2", target_bir_lowering=False)
    stack = contextlib.ExitStack()

    x_d = nc.dram_tensor("x", [B, S, D], F32, kind="ExternalInput")
    pw_d = nc.dram_tensor("pw", [D], F32, kind="ExternalInput")
    wa_d = nc.dram_tensor("wa", [D, D], F32, kind="ExternalInput")
    wi_d = nc.dram_tensor("wi", [D, D], F32, kind="ExternalInput")
    wv_d = nc.dram_tensor("wv", [D, D], F32, kind="ExternalInput")
    wop_d = nc.dram_tensor("wop", [D, D], F32, kind="ExternalInput")
    db_d = nc.dram_tensor("db", [D], F32, kind="ExternalInput")
    fw_d = nc.dram_tensor("fw", [D], F32, kind="ExternalInput")
    wgu_d = nc.dram_tensor("wgu", [D, 2 * FF], F32, kind="ExternalInput")
    wdn_d = nc.dram_tensor("wdn", [FF, D], F32, kind="ExternalInput")
    sw_d = nc.dram_tensor("sw", [8, S * D], F32, kind="ExternalInput")
    sb_d = nc.dram_tensor("sb", [8], F32, kind="ExternalInput")
    w1_d = nc.dram_tensor("w1", [8, 128, 128], F32, kind="ExternalInput")
    b1_d = nc.dram_tensor("b1", [8, 128], F32, kind="ExternalInput")
    w2_d = nc.dram_tensor("w2", [8, 128, 128], F32, kind="ExternalInput")
    b2_d = nc.dram_tensor("b2", [8, 128], F32, kind="ExternalInput")
    w3_d = nc.dram_tensor("w3", [D, D], F32, kind="ExternalInput")
    b3_d = nc.dram_tensor("b3", [D], F32, kind="ExternalInput")
    out_d = nc.dram_tensor("out", [B, S, D], F32, kind="ExternalOutput")

    ik = "ExternalOutput" if dbg else "Internal"
    s_xn = nc.dram_tensor("s_xn", [B, S, D], F32, kind=ik)
    s_xnb = nc.dram_tensor("s_xnb", [B, S, D], BF16, kind="Internal")
    s_h = nc.dram_tensor("s_h", [B, S, D], BF16, kind=ik)
    s_y = nc.dram_tensor("s_y", [B, NG, NK, 128, GTOK], BF16, kind=ik)
    s_x2 = nc.dram_tensor("s_x2", [B, S, D], F32, kind=ik)
    s_h2 = nc.dram_tensor("s_h2", [B, S, D], BF16, kind="Internal")
    s_z = nc.dram_tensor("s_z", [B, NG, NF, 128, GTOK], BF16, kind=ik)
    s_s = nc.dram_tensor("s_s", [B, S, D], F32, kind=ik)
    s_w1b = nc.dram_tensor("s_w1b", [8, 128, 128], BF16, kind="Internal")
    s_w2b = nc.dram_tensor("s_w2b", [8, 128, 128], BF16, kind="Internal")
    s_w3b = nc.dram_tensor("s_w3b", [D, D], BF16, kind="Internal")
    d_gate = nc.dram_tensor("d_gate", [128, 8 * B], F32,
                            kind=ik) if dbg else None

    T = Stream("T")
    V = Stream("V")
    SC = Stream("SC")
    Q = Stream("Q")
    G = Stream("G")
    streams = [T, V, SC, Q, G]

    def mksem(name):
        return Sem(stack, nc, name)

    cmt = mksem("cmt")
    cmv = mksem("cmv")
    cms = mksem("cms")
    dx = mksem("dx")
    dsp = mksem("dsp")
    dgp = mksem("dgp")
    dtr = mksem("dtr")
    dwl3 = [mksem(f"dwl{i}") for i in range(3)]
    dwlp = [mksem(f"dwlp{i}") for i in range(2)]
    dwop = mksem("dwop")
    dwdn = mksem("dwdn")
    dout = mksem("dout")
    cpe = [mksem(f"cpe{i}") for i in range(4)]

    sbt = lambda name, shape, dt: stack.enter_context(nc.sbuf_tensor(name, shape, dt))
    ones1 = sbt("ones1", [1, 128], F32)
    ones64 = sbt("ones64", [128, 64], F32)
    ndb = sbt("ndb", [128, NK], F32)
    stc = sbt("stc", [128, NK], F32)
    pwb = sbt("pwb", [128, D], F32)
    rowbuf = sbt("rowbuf", [1, 512], F32)
    racc = sbt("racc", [128, 8 * B], F32)
    junk = sbt("junk", [128, D], F32)
    colp = sbt("colp", [128, 16], F32)
    lgrow = sbt("lgrow", [1, 16], F32)
    sbrow = sbt("sbrow", [1, 8], F32)
    lgb = sbt("lgb", [128, 8 * B], F32)
    gcol = sbt("gcol", [128, 8 * B], F32)
    b1t = sbt("b1t", [128, 8], F32)
    b2t = sbt("b2t", [128, 8], F32)
    b2mix = sbt("b2mix", [128, B], F32)
    tmp8 = sbt("tmp8", [128, 8], F32)

    V.emit(lambda e: e.memset(ones1[:, :], 1.0))
    V.emit(lambda e: e.memset(ones64[:, :], 1.0))
    V.emit(lambda e: e.memset(racc[:, :], 0.0))
    V.emit(lambda e: e.memset(stc[:, :], 0.0), inc=cmv)

    Q.emit(lambda e: e.dma_start(ndb[:, :], db_d.rearrange("(k p) -> p k", p=128),
                                 allow_slow_non_contiguous=True),
           inc=dx, inc_n=16)
    for ei in range(8):
        Q.emit(lambda e, ei=ei: e.dma_start(b1t[:, ei:ei + 1], b1_d[ei, :, None],
                                            allow_slow_non_contiguous=True),
               inc=dx, inc_n=16)
        Q.emit(lambda e, ei=ei: e.dma_start(b2t[:, ei:ei + 1], b2_d[ei, :, None],
                                            allow_slow_non_contiguous=True),
               inc=dx, inc_n=16)
    Q.emit(lambda e: e.dma_start(sbrow[:, :], sb_d[None, :]), inc=dx, inc_n=16)
    V.wait(dx, 16)
    V.emit(lambda e: e.tensor_scalar_mul(ndb[:, :], ndb[:, :], -1.0), inc=cmv)

    with nc.psum_tensor("psb", [128, 512], F32) as psb:
        T.wait(cmv, cmv.n)
        for n in range(4):
            if n > 0:
                Q.wait(cmt, cmt.n)
            Q.emit(lambda e, n=n: e.dma_start(rowbuf[:, :],
                                              pw_d[None, 512 * n:512 * (n + 1)]),
                   inc=dx, inc_n=16)
            T.wait(dx, dx.n)
            T.wait(cmv, cmv.n)
            T.emit(lambda e, n=n: e.matmul(psb[:, :], ones1[:, :], rowbuf[:, :],
                                           start=True, stop=True), inc=cmt)
            V.wait(cmt, cmt.n)
            V.emit(lambda e, n=n: e.tensor_copy(pwb[:, 512 * n:512 * (n + 1)],
                                                psb[:, :]), inc=cmv)
    pwb_done = cmv.n

    G.emit(lambda e: e.dma_start(s_w1b[:, :, :], w1_d[:, :, :]), inc=dgp, inc_n=16)
    G.emit(lambda e: e.dma_start(s_w2b[:, :, :], w2_d[:, :, :]), inc=dgp, inc_n=16)
    G.emit(lambda e: e.dma_start(s_w3b[:, :], w3_d[:, :]), inc=dgp, inc_n=16)

    groups = [(b, gi) for b in range(B) for gi in range(NG)]
    xn_done = {}
    h_spill = {}
    hT_done = {}
    chain_mm = {}
    npool = [0, 0, 0, 0]
    wl_target = {}
    xnbb_sp = {}
    unit_last_v = {}
    unit_last_s = {}
    wmats = [wa_d, wi_d, wv_d]

    p1 = {}
    p1_stack = contextlib.ExitStack()

    def p1_alloc():
        es = p1_stack.enter_context
        p1["hTg"] = es(nc.sbuf_tensor("hTg", [128, 2 * NK * GTOK], BF16))
        p1["yg"] = es(nc.sbuf_tensor("yg", [128, NK * GTOK], BF16))
        p1["xsb"] = es(nc.sbuf_tensor("xsb", [128, 2 * D], F32))
        p1["hsb"] = es(nc.sbuf_tensor("hsb", [128, 2 * D], BF16))
        p1["xnbb"] = es(nc.sbuf_tensor("xnbb", [128, D], BF16))
        p1["wsb"] = es(nc.sbuf_tensor("wsb", [128, 3 * NK * 128], BF16))
        p1["sc"] = es(nc.sbuf_tensor("scp", [128, 12 * 1024], F32))
        for j in range(4):
            p1[f"pp{j}"] = es(nc.psum_tensor(f"pp{j}", [128, 1024], F32))

    p1_alloc()

    def stage_a(b, gi, i):
        t0 = gi * GTOK + i * TIL
        par = i % 2
        xt = p1["xsb"][:, par * D:(par + 1) * D]
        ht = p1["hsb"][:, par * D:(par + 1) * D]
        gidx = groups.index((b, gi))
        prev = None
        if i >= 2:
            prev = (b, gi, i - 2)
        elif gidx > 0:
            pb, pg = groups[gidx - 1]
            prev = (pb, pg, GT - 2 + i)
        if prev is not None:
            Q.wait(dsp, xn_done[prev][1])
        Q.emit(lambda e: e.dma_start(xt[:], x_d[b, t0:t0 + TIL, :]), inc=dx, inc_n=16)
        xld = dx.n
        SC.wait(dx, xld)
        c0 = colp[:, par * 5 + 0:par * 5 + 1]
        c2 = colp[:, par * 5 + 2:par * 5 + 3]
        c3 = colp[:, par * 5 + 3:par * 5 + 4]
        c4 = colp[:, par * 5 + 4:par * 5 + 5]
        SC.emit(lambda e: e.activation(junk[:, :], xt[:], AF.Copy, accum_out=c0),
                inc=cms)
        V.wait(cms, cms.n)
        V.emit(lambda e: e.tensor_scalar_mul(c0, c0, -1.0 / D), inc=cmv)
        V.wait(cmv, cmv.n)
        V.emit(lambda e: e.tensor_scalar_add(xt[:], xt[:], c0), inc=cmv)
        SC.wait(cmv, cmv.n)
        SC.emit(lambda e: e.activation(junk[:, :], xt[:], AF.Square, accum_out=c2),
                inc=cms)
        V.wait(cms, cms.n)
        V.emit(lambda e: e.tensor_scalar(c3, c2, 1.0 / D, 1e-5, ALU.mult, ALU.add),
               inc=cmv)
        SC.wait(cmv, cmv.n)
        SC.emit(lambda e: e.activation(c3, c3, AF.Ln), inc=cms)
        SC.wait(cms, cms.n)
        SC.emit(lambda e: e.activation(c3, c3, AF.Exp, scale=-0.5), inc=cms)
        V.wait(cms, cms.n)
        V.emit(lambda e: e.tensor_scalar(c4, c2, 1.0 / D, c3, ALU.mult, ALU.mult),
               inc=cmv)
        V.wait(cmv, cmv.n)
        V.emit(lambda e: e.tensor_scalar(c4, c4, c3, 1e-6, ALU.mult, ALU.add),
               inc=cmv)
        SC.wait(cmv, cmv.n)
        SC.emit(lambda e: e.activation(c4, c4, AF.Ln), inc=cms)
        SC.wait(cms, cms.n)
        SC.emit(lambda e: e.activation(c4, c4, AF.Exp, scale=-0.5), inc=cms)
        V.wait(cms, cms.n)
        V.emit(lambda e: e.tensor_scalar_mul(xt[:], xt[:], c3), inc=cmv)
        xnv = cmv.n
        if prev is not None:
            V.wait(dsp, xn_done[prev][1])
        V.emit(lambda e: e.scalar_tensor_tensor(ht[:], xt[:], c4, pwb[:, :],
                                                ALU.mult, ALU.mult), inc=cmv)
        hv = cmv.n
        SC.wait(cmv, hv)
        SC.emit(lambda e: e.dma_start(s_xn[b, t0:t0 + TIL, :], xt[:]),
                inc=dsp, inc_n=16)
        SC.emit(lambda e: e.dma_start(s_h[b, t0:t0 + TIL, :], ht[:]),
                inc=dsp, inc_n=16)
        gti = groups.index((b, gi)) * GT + i
        if gti > 0:
            V.wait(dsp, xnbb_sp[gti - 1])
        V.emit(lambda e: e.tensor_copy(p1["xnbb"][:, :], xt[:]), inc=cmv)
        SC.wait(cmv, cmv.n)
        SC.emit(lambda e, b=b, t0=t0: e.dma_start(s_xnb[b, t0:t0 + TIL, :],
                                                  p1["xnbb"][:, :]),
                inc=dsp, inc_n=16)
        xnbb_sp[gti] = dsp.n
        xn_sp = dsp.n
        xn_done[(b, gi, i)] = (xnv, xn_sp)

    def transposes(b, gi):
        gidx = groups.index((b, gi))
        gbuf = gidx % 2
        Q.wait(dsp, h_spill[(b, gi)])
        if gidx >= 2:
            pb, pg = groups[gidx - 2]
            Q.wait(cmt, chain_mm[(pb, pg, 3 * NK - 1)])
        for k in range(NK):
            dst = p1["hTg"][:, gbuf * NK * GTOK + k * GTOK:
                            gbuf * NK * GTOK + (k + 1) * GTOK]
            src = s_h[b, gi * GTOK:(gi + 1) * GTOK, k * 128:(k + 1) * 128]
            Q.emit(lambda e, d=dst, s=src: e.dma_start(d, s, transpose=True),
                   inc=dtr, inc_n=16)
        hT_done[(b, gi)] = dtr.n

    def chain_cmt(gci):
        return cmt_base + gci + 1

    def weight_loads(b, gi):
        gidx = groups.index((b, gi))
        for ci in range(3 * NK):
            m, wj = divmod(ci, 3)
            buf = ci % 3
            gci = gidx * 3 * NK + ci
            if gci >= 3:
                G.wait(cmt, chain_cmt(gci - 3))
            dst = p1["wsb"][:, buf * NK * 128:(buf + 1) * NK * 128].rearrange(
                "p (k c) -> p k c", c=128)
            src = wmats[wj][:, m * 128:(m + 1) * 128].rearrange(
                "(k p) c -> p k c", p=128)
            G.emit(lambda e, d=dst, s=src: e.dma_start(d, s),
                   inc=dwl3[buf], inc_n=16)
            wl_target[(b, gi, ci)] = (buf, dwl3[buf].n)

    def mm_chains(b, gi):
        gidx = groups.index((b, gi))
        gbuf = gidx % 2
        T.wait(dtr, hT_done[(b, gi)])
        for ci in range(3 * NK):
            m, wj = divmod(ci, 3)
            buf = ci % 3
            pool = ci % 4
            _wb, _wv = wl_target[(b, gi, ci)]
            T.wait(dwl3[_wb], _wv)
            if npool[pool] > 0:
                T.wait(cpe[pool], npool[pool])
            ps = p1[f"pp{pool}"]
            wt = p1["wsb"][:, buf * NK * 128:(buf + 1) * NK * 128]
            for k in range(NK):
                for hh in range(2):
                    last = (k == NK - 1) and (hh == 1)
                    rhs = p1["hTg"][:, gbuf * NK * GTOK + k * GTOK + hh * 512:
                                    gbuf * NK * GTOK + k * GTOK + hh * 512 + 512]
                    T.emit(lambda e, ps=ps, w=wt[:, k * 128:(k + 1) * 128], r=rhs,
                           hh=hh, k=k:
                           e.matmul(ps[:, hh * 512:hh * 512 + 512], w, r,
                                    start=(k == 0), stop=(k == NK - 1)),
                           inc=cmt if last else None)
            chain_mm[(b, gi, ci)] = cmt.n
            assert cmt.n == cmt_base + gidx * 3 * NK + ci + 1
            npool[pool] += 1

    def sphase_a(b, gi, m):
        pool = m % 2
        bs = [p1["sc"][:, (pool * 6 + j) * 1024:(pool * 6 + j + 1) * 1024]
              for j in range(6)]
        prev = (b, gi, m - 2)
        if m < 2:
            gidx = groups.index((b, gi))
            if gidx > 0:
                pb, pg = groups[gidx - 1]
                prev = (pb, pg, NK - 2 + m)
            else:
                prev = None
        if prev is not None:
            SC.wait(cmv, unit_last_v[prev])
        SC.wait(cmt, chain_mm[(b, gi, 3 * m + 1)])
        pa = p1[f"pp{(3 * m) % 4}"]
        pi_ = p1[f"pp{(3 * m + 1) % 4}"]
        ndbc = ndb[:, m:m + 1]
        SC.emit(lambda e: e.activation(bs[0][:, :], pa[:, :], AF.Exp, bias=ndbc,
                                       scale=-1.0))
        SC.emit(lambda e: e.activation(bs[1][:, :], pi_[:, :], AF.Exp, scale=-1.0),
                inc=cms)
        SC.emit(lambda e: e.activation(colp[:, 10:11], colp[:, 10:11], AF.Copy),
                inc=cpe[(3 * m) % 4])
        SC.emit(lambda e: e.activation(colp[:, 10:11], colp[:, 10:11], AF.Copy),
                inc=cpe[(3 * m + 1) % 4])
        return cms.n

    def vphase_a(b, gi, m, sa):
        pool = m % 2
        bs = [p1["sc"][:, (pool * 6 + j) * 1024:(pool * 6 + j + 1) * 1024]
              for j in range(6)]
        prev = (b, gi, m - 2)
        if m < 2:
            gidx = groups.index((b, gi))
            if gidx > 0:
                pb, pg = groups[gidx - 1]
                prev = (pb, pg, NK - 2 + m)
            else:
                prev = None
        if prev is not None:
            V.wait(cms, unit_last_s[prev])
        V.wait(cms, sa)
        V.wait(cmt, chain_mm[(b, gi, 3 * m + 2)])
        pv = p1[f"pp{(3 * m + 2) % 4}"]
        V.emit(lambda e: e.tensor_copy(bs[2][:, :], pv[:, :]), inc=cpe[(3 * m + 2) % 4])
        V.emit(lambda e: e.tensor_scalar_add(bs[3][:, :], bs[0][:, :], 1.0))
        V.emit(lambda e: e.tensor_scalar_add(bs[4][:, :], bs[1][:, :], 1.0), inc=cmv)
        return cmv.n

    def sphase_b(b, gi, m, va):
        pool = m % 2
        bs = [p1["sc"][:, (pool * 6 + j) * 1024:(pool * 6 + j + 1) * 1024]
              for j in range(6)]
        SC.wait(cmv, va)
        SC.emit(lambda e: e.activation(bs[0][:, :], bs[3][:, :], AF.Ln))
        SC.emit(lambda e: e.activation(bs[1][:, :], bs[4][:, :], AF.Ln), inc=cms)
        return cms.n

    def vphase_b(b, gi, m, sb_):
        pool = m % 2
        bs = [p1["sc"][:, (pool * 6 + j) * 1024:(pool * 6 + j + 1) * 1024]
              for j in range(6)]
        V.wait(cms, sb_)
        V.emit(lambda e: e.tensor_scalar_min(bs[3][:, :], bs[0][:, :], CLAMP))
        V.emit(lambda e: e.memset(bs[0][:, :], 1.0))
        V.emit(lambda e: e.memset(
            bass.AP(bs[0].tensor, bs[0].offset + 64,
                    [bs[0].ap[0], [64, NCH - 1]]), 0.0), inc=cmv)
        V.wait(cmv, cmv.n)
        V.emit(lambda e: e.tensor_tensor_scan(bs[4][:, :], bs[0][:, :],
                                              bs[3][:, :], 0.0,
                                              ALU.mult, ALU.add))
        V.emit(lambda e: e.tensor_scalar_min(bs[5][:, :], bs[4][:, :], CLAMP),
               inc=cmv)
        return cmv.n

    def sphase_c(b, gi, m, vb):
        pool = m % 2
        bs = [p1["sc"][:, (pool * 6 + j) * 1024:(pool * 6 + j + 1) * 1024]
              for j in range(6)]
        SC.wait(cmv, vb)
        SC.emit(lambda e: e.activation(bs[0][:, :], bs[3][:, :], AF.Exp, scale=-1.0))
        SC.emit(lambda e: e.activation(bs[3][:, :], bs[4][:, :], AF.Exp, scale=-1.0))
        SC.emit(lambda e: e.activation(bs[5][:, :], bs[5][:, :], AF.Exp), inc=cms)
        return cms.n

    def vphase_c(b, gi, m, sc_):
        pool = m % 2
        bs = [p1["sc"][:, (pool * 6 + j) * 1024:(pool * 6 + j + 1) * 1024]
              for j in range(6)]
        V.wait(cms, sc_)
        V.emit(lambda e: e.tensor_tensor(bs[0][:, :], bs[0][:, :], bs[0][:, :],
                                         ALU.mult))
        V.emit(lambda e: e.tensor_scalar(bs[0][:, :], bs[0][:, :], -1.0, 1.0,
                                         ALU.mult, ALU.add))
        V.emit(lambda e: e.tensor_scalar_max(bs[0][:, :], bs[0][:, :], 1e-8),
               inc=cmv)
        return cmv.n

    def sphase_d(b, gi, m, vc):
        pool = m % 2
        bs = [p1["sc"][:, (pool * 6 + j) * 1024:(pool * 6 + j + 1) * 1024]
              for j in range(6)]
        SC.wait(cmv, vc)
        SC.emit(lambda e: e.activation(bs[0][:, :], bs[0][:, :], AF.Ln), inc=cms)
        return cms.n

    def vphase_d(b, gi, m, sd):
        pool = m % 2
        bs = [p1["sc"][:, (pool * 6 + j) * 1024:(pool * 6 + j + 1) * 1024]
              for j in range(6)]
        V.wait(cms, sd)
        V.emit(lambda e: e.scalar_tensor_tensor(bs[1][:, :], bs[0][:, :], 0.5,
                                                bs[1][:, :], ALU.mult, ALU.subtract),
               inc=cmv)
        return cmv.n

    def sphase_e(b, gi, m, vd):
        pool = m % 2
        bs = [p1["sc"][:, (pool * 6 + j) * 1024:(pool * 6 + j + 1) * 1024]
              for j in range(6)]
        SC.wait(cmv, vd)
        SC.emit(lambda e: e.activation(bs[1][:, :], bs[1][:, :], AF.Exp), inc=cms)
        unit_last_s[(b, gi, m)] = cms.n
        return cms.n

    def vphase_e(b, gi, m, se):
        pool = m % 2
        bs = [p1["sc"][:, (pool * 6 + j) * 1024:(pool * 6 + j + 1) * 1024]
              for j in range(6)]
        V.wait(cms, se)
        V.emit(lambda e: e.tensor_tensor(bs[2][:, :], bs[1][:, :], bs[2][:, :],
                                         ALU.mult))
        V.emit(lambda e: e.tensor_tensor(bs[2][:, :], bs[2][:, :], bs[5][:, :],
                                         ALU.mult))
        stcol = stc[:, m:m + 1]
        V.emit(lambda e: e.memset(bs[0][:, :], 1.0), inc=cmv)
        V.wait(cmv, cmv.n)
        V.emit(lambda e: e.tensor_copy(
            bass.AP(bs[0].tensor, bs[0].offset + 64, [bs[0].ap[0], [64, NCH - 1]]),
            bass.AP(bs[3].tensor, bs[3].offset + 63, [bs[3].ap[0], [64, NCH - 1]])),
            inc=cmv)
        V.wait(cmv, cmv.n)
        V.emit(lambda e: e.tensor_tensor_scan(bs[4][:, :], bs[0][:, :],
                                              bs[2][:, :], stcol,
                                              ALU.mult, ALU.add))
        ydst = p1["yg"][:, m * GTOK:(m + 1) * GTOK]
        V.emit(lambda e: e.tensor_tensor(ydst, bs[3][:, :], bs[4][:, :], ALU.mult))
        V.emit(lambda e: e.tensor_tensor(stcol, bs[3][:, 1023:1024],
                                         bs[4][:, 1023:1024], ALU.mult), inc=cmv)
        unit_last_v[(b, gi, m)] = cmv.n
        return cmv.n

    def scan_pair(b, gi, m0):
        ms = [m0, m0 + 1]
        sa = [sphase_a(b, gi, m) for m in ms]
        va = [vphase_a(b, gi, m, sa[j]) for j, m in enumerate(ms)]
        sb_ = [sphase_b(b, gi, m, va[j]) for j, m in enumerate(ms)]
        vb = [vphase_b(b, gi, m, sb_[j]) for j, m in enumerate(ms)]
        sc_ = [sphase_c(b, gi, m, vb[j]) for j, m in enumerate(ms)]
        vc = [vphase_c(b, gi, m, sc_[j]) for j, m in enumerate(ms)]
        sd = [sphase_d(b, gi, m, vc[j]) for j, m in enumerate(ms)]
        vd = [vphase_d(b, gi, m, sd[j]) for j, m in enumerate(ms)]
        se = [sphase_e(b, gi, m, vd[j]) for j, m in enumerate(ms)]
        for j, m in enumerate(ms):
            vphase_e(b, gi, m, se[j])

    cmt_base = cmt.n
    y_spill = {}
    for gidx, (b, gi) in enumerate(groups):
        if gidx == 0:
            V.wait(cmv, pwb_done)
            T.wait(cmv, pwb_done)
            for i in range(GT):
                stage_a(b, gi, i)
            h_spill[(b, gi)] = dsp.n
            transposes(b, gi)
        weight_loads(b, gi)
        if gi == 0 and b > 0:
            V.emit(lambda e: e.memset(stc[:, :], 0.0), inc=cmv)
        mm_chains(b, gi)
        if gidx + 1 < len(groups):
            nb, ngi = groups[gidx + 1]
        else:
            nb = None
        if gidx > 0:
            V.wait(dsp, y_spill[groups[gidx - 1]])
        for mp in range(NK // 2):
            scan_pair(b, gi, 2 * mp)
            if nb is not None and mp < GT // 2:
                stage_a(nb, ngi, 2 * mp)
                stage_a(nb, ngi, 2 * mp + 1)
                if mp == GT // 2 - 1:
                    h_spill[(nb, ngi)] = dsp.n
                    transposes(nb, ngi)
        SC.wait(cmv, cmv.n)
        SC.emit(lambda e, b=b, gi=gi: e.dma_start(
            s_y[b, gi].rearrange("k p t -> p k t"),
            p1["yg"][:, :].rearrange("p (k t) -> p k t", k=NK)),
            inc=dsp, inc_n=16)
        y_spill[(b, gi)] = dsp.n


    p1_stack.close()
    p1cmt, p1cmv, p1cms, p1dsp = cmt.n, cmv.n, cms.n, dsp.n

    p2_stack = contextlib.ExitStack()
    sb2 = lambda name, shape, dt: p2_stack.enter_context(
        nc.sbuf_tensor(name, shape, dt))
    wopsb = sb2("wopsb", [128, NK * 2048], BF16)
    fwb = sb2("fwb", [128, D], F32)
    ysb = sb2("ysb", [128, 2 * 2048], BF16)
    xnb2 = sb2("xnb2", [128, 2 * D], F32)
    h2b = sb2("h2b", [128, 2 * D], BF16)
    psW2 = [p2_stack.enter_context(nc.psum_tensor(f"psW2{j}", [128, 2048], F32))
            for j in range(2)]

    G.wait(cmt, p1cmt)
    G.wait(cmv, p1cmv)
    Q.wait(cmt, p1cmt)
    Q.wait(cmv, p1cmv)
    Q.wait(dsp, p1dsp)

    for kb in range(4):
        src = wop_d[kb * 512:(kb + 1) * 512, :].rearrange("(k p) n -> p k n", p=128)
        dst = wopsb[:, kb * 4 * 2048:(kb + 1) * 4 * 2048].rearrange(
            "p (k n) -> p k n", k=4)
        G.emit(lambda e, d=dst, s=src: e.dma_start(d, s), inc=dwop, inc_n=16)
    wop_ld = dwop.n

    T.wait(cmv, p1cmv)
    for n in range(4):
        if n > 0:
            Q.wait(cmt, cmt.n)
        Q.emit(lambda e, n=n: e.dma_start(rowbuf[:, :],
                                          fw_d[None, 512 * n:512 * (n + 1)]),
               inc=dx, inc_n=16)
        T.wait(dx, dx.n)
        T.emit(lambda e, n=n: e.matmul(psW2[0][:, 512 * n:512 * (n + 1)],
                                       ones1[:, :], rowbuf[:, :],
                                       start=True, stop=True), inc=cmt)
    V.wait(cmt, cmt.n)
    V.emit(lambda e: e.tensor_copy(fwb[:, :], psW2[0][:, :]), inc=cmv)
    fwb_done = cmv.n
    T.wait(cmv, fwb_done)

    p2_free = {}
    for b in range(B):
        for i in range(NT):
            t0 = i * TIL
            par = i % 2
            gi = i // GT
            ti = i % GT
            gidx2 = b * NT + i
            yt = ysb[:, par * 2048:(par + 1) * 2048]
            xnt = xnb2[:, par * D:(par + 1) * D]
            h2t = h2b[:, par * D:(par + 1) * D]
            if gidx2 >= 2:
                Q.wait(dsp, p2_free[gidx2 - 2][1])
            Q.wait(dsp, y_spill[(b, gi)])
            src = s_y[b, gi, :, :, ti * 128:(ti + 1) * 128].rearrange(
                "k p t -> p k t")
            Q.emit(lambda e, d=yt.rearrange("p (k t) -> p k t", k=NK), s=src:
                   e.dma_start(d, s), inc=dx, inc_n=16)
            Q.emit(lambda e, d=xnt, s=s_xn[b, t0:t0 + TIL, :]:
                   e.dma_start(d, s[:, :]), inc=dx, inc_n=16)
            ld = dx.n
            T.wait(dx, ld)
            if gidx2 == 0:
                T.wait(dwop, wop_ld)
            if gidx2 >= 2:
                T.wait(cmv, p2_free[gidx2 - 2][0])
            ps = psW2[par]
            for n in range(4):
                for k in range(NK):
                    T.emit(lambda e, ps=ps, n=n, k=k, yt=yt:
                           e.matmul(ps[:, n * 512:(n + 1) * 512],
                                    yt[:, k * 128:(k + 1) * 128],
                                    wopsb[:, k * 2048 + n * 512:
                                          k * 2048 + (n + 1) * 512],
                                    start=(k == 0), stop=(k == NK - 1)),
                           inc=cmt if (n == 3 and k == NK - 1) else None)
            V.wait(cmt, cmt.n)
            V.emit(lambda e, ps=ps, xnt=xnt:
                   e.tensor_tensor(xnt[:], ps[:, :], xnt[:], ALU.add), inc=cmv)
            x2v = cmv.n
            c5 = colp[:, 10 + par:11 + par]
            SC.wait(cmv, x2v)
            SC.emit(lambda e, xnt=xnt, c=c5:
                    e.activation(junk[:, :], xnt[:], AF.Square, accum_out=c),
                    inc=cms)
            V.wait(cms, cms.n)
            V.emit(lambda e, c=c5: e.tensor_scalar(c, c, 1.0 / D, 1e-6,
                                                   ALU.mult, ALU.add), inc=cmv)
            SC.wait(cmv, cmv.n)
            SC.emit(lambda e, c=c5: e.activation(c, c, AF.Ln), inc=cms)
            SC.wait(cms, cms.n)
            SC.emit(lambda e, c=c5: e.activation(c, c, AF.Exp, scale=-0.5), inc=cms)
            V.wait(cms, cms.n)
            if gidx2 >= 2:
                V.wait(dsp, p2_free[gidx2 - 2][1])
            V.emit(lambda e, o=h2t, i_=xnt, c=c5:
                   e.scalar_tensor_tensor(o[:], i_[:], c, fwb[:, :],
                                          ALU.mult, ALU.mult), inc=cmv)
            h2v = cmv.n
            SC.wait(cmv, h2v)
            SC.emit(lambda e, s=xnt, b=b, t0=t0:
                    e.dma_start(s_x2[b, t0:t0 + TIL, :], s[:]), inc=dsp, inc_n=16)
            SC.emit(lambda e, s=h2t, b=b, t0=t0:
                    e.dma_start(s_h2[b, t0:t0 + TIL, :], s[:]), inc=dsp, inc_n=16)
            p2_free[gidx2] = (x2v, dsp.n)

    p2_stack.close()
    p2cmt, p2cmv, p2cms, p2dsp = cmt.n, cmv.n, cms.n, dsp.n

    p3_stack = contextlib.ExitStack()
    sb3 = lambda name, shape, dt: p3_stack.enter_context(
        nc.sbuf_tensor(name, shape, dt))
    h2Tg = sb3("h2Tg", [128, 2 * NK * GTOK], BF16)
    wgsb = sb3("wgsb", [128, 2 * NK * 128], BF16)
    wusb = sb3("wusb", [128, 2 * NK * 128], BF16)
    zgsb = sb3("zgsb", [128, 2 * GTOK], F32)
    zbsb = sb3("zbsb", [128, 2 * GTOK], BF16)
    swsb = sb3("swsb", [128, 2 * D], F32)
    xn3 = sb3("xn3", [128, 2 * D], F32)
    psG = [p3_stack.enter_context(nc.psum_tensor(f"psG{j}", [128, GTOK], F32))
           for j in range(2)]
    psU = [p3_stack.enter_context(nc.psum_tensor(f"psU{j}", [128, GTOK], F32))
           for j in range(2)]

    G.wait(cmt, p2cmt)
    G.wait(cmv, p2cmv)
    Q.wait(cmt, p2cmt)
    Q.wait(cmv, p2cmv)
    Q.wait(dsp, p2dsp)

    p3_mp_done = {}
    p3_mp_free = {}
    p3_groups = [(b, gi) for b in range(B) for gi in range(NG)]
    h2T_done3 = {}
    zsp = {}
    rt_v = {}
    rt_done = {}

    for g3, (b, gi) in enumerate(p3_groups):
        gbuf = g3 % 2
        if g3 >= 2:
            Q.wait(cmt, p3_mp_done[(g3 - 2, NF - 1)])
        for k in range(NK):
            dst = h2Tg[:, gbuf * NK * GTOK + k * GTOK:
                       gbuf * NK * GTOK + (k + 1) * GTOK]
            src = s_h2[b, gi * GTOK:(gi + 1) * GTOK, k * 128:(k + 1) * 128]
            Q.emit(lambda e, d=dst, s=src: e.dma_start(d, s, transpose=True),
                   inc=dtr, inc_n=16)
        h2T_done3[g3] = dtr.n
        for mp in range(NF):
            par = mp % 2
            gmp = g3 * NF + mp
            if gmp >= 2:
                pg3, pmp = divmod(gmp - 2, NF)
                G.wait(cmt, p3_mp_done[(pg3, pmp)])
            wg_src = wgu_d[:, mp * 128:(mp + 1) * 128].rearrange(
                "(k p) c -> p k c", p=128)
            wu_src = wgu_d[:, FF + mp * 128:FF + (mp + 1) * 128].rearrange(
                "(k p) c -> p k c", p=128)
            wg_dst = wgsb[:, par * NK * 128:(par + 1) * NK * 128]
            wu_dst = wusb[:, par * NK * 128:(par + 1) * NK * 128]
            G.emit(lambda e, d=wg_dst.rearrange("p (k c) -> p k c", c=128),
                   s=wg_src: e.dma_start(d, s), inc=dwlp[par], inc_n=16)
            G.emit(lambda e, d=wu_dst.rearrange("p (k c) -> p k c", c=128),
                   s=wu_src: e.dma_start(d, s), inc=dwlp[par], inc_n=16)
            T.wait(dwlp[par], dwlp[par].n)
            if g3 == 0 and mp == 0:
                T.wait(dtr, h2T_done3[g3])
            if mp == 0 and g3 > 0:
                T.wait(dtr, h2T_done3[g3])
            if gmp >= 2:
                pg3, pmp = divmod(gmp - 2, NF)
                T.wait(cms, p3_mp_free[(pg3, pmp)][0])
                T.wait(cmv, p3_mp_free[(pg3, pmp)][1])
            for k in range(NK):
                for hh in range(2):
                    rhs = h2Tg[:, gbuf * NK * GTOK + k * GTOK + hh * 512:
                               gbuf * NK * GTOK + k * GTOK + hh * 512 + 512]
                    T.emit(lambda e, ps=psG[par], w=wg_dst[:, k * 128:(k + 1) * 128],
                           r=rhs, hh=hh, k=k:
                           e.matmul(ps[:, hh * 512:hh * 512 + 512], w, r,
                                    start=(k == 0), stop=(k == NK - 1)),
                           inc=cmt if (k == NK - 1 and hh == 1) else None)
            gate_mm = cmt.n
            for k in range(NK):
                for hh in range(2):
                    rhs = h2Tg[:, gbuf * NK * GTOK + k * GTOK + hh * 512:
                               gbuf * NK * GTOK + k * GTOK + hh * 512 + 512]
                    T.emit(lambda e, ps=psU[par], w=wu_dst[:, k * 128:(k + 1) * 128],
                           r=rhs, hh=hh, k=k:
                           e.matmul(ps[:, hh * 512:hh * 512 + 512], w, r,
                                    start=(k == 0), stop=(k == NK - 1)),
                           inc=cmt if (k == NK - 1 and hh == 1) else None)
            p3_mp_done[(g3, mp)] = cmt.n
            zg = zgsb[:, par * GTOK:(par + 1) * GTOK]
            zb = zbsb[:, par * GTOK:(par + 1) * GTOK]
            SC.wait(cmt, gate_mm)
            SC.emit(lambda e, o=zg, i_=psG[par]: e.activation(o[:], i_[:, :], AF.Silu),
                    inc=cms)
            silu_v = cms.n
            V.wait(cmt, p3_mp_done[(g3, mp)])
            V.wait(cms, silu_v)
            if gmp >= 2:
                V.wait(dsp, zsp[(gmp - 2) // NF, (gmp - 2) % NF]
                       if False else zsp[divmod(gmp - 2, NF)])
            V.emit(lambda e, o=zb, a=zg, bb=psU[par]:
                   e.tensor_tensor(o[:], a[:], bb[:, :], ALU.mult), inc=cmv)
            mult_v = cmv.n
            p3_mp_free[(g3, mp)] = (silu_v, mult_v)
            SC.wait(cmv, mult_v)
            SC.emit(lambda e, s=zb, b=b, gi=gi, mp=mp:
                    e.dma_start(s_z[b, gi, mp], s[:]), inc=dsp, inc_n=16)
            zsp[(g3, mp)] = dsp.n
            ti, ei = divmod(mp, 8)
            rpar = ti % 2
            xnt3 = xn3[:, rpar * D:(rpar + 1) * D]
            swt = swsb[:, par * D:(par + 1) * D]
            if ei == 0:
                prev_rt = (g3, ti - 2) if ti >= 2 else ((g3 - 1, ti + GT - 2)
                                                        if g3 > 0 else None)
                if prev_rt is not None and prev_rt in rt_v:
                    Q.wait(cmv, rt_v[prev_rt])
                Q.emit(lambda e, d=xnt3, s=s_xn[b, gi * GTOK + ti * 128:
                                                gi * GTOK + (ti + 1) * 128, :]:
                       e.dma_start(d[:], s[:, :]), inc=dx, inc_n=16)
            prev_sw = gmp - 2
            if prev_sw >= 0:
                pg3, pmp = divmod(prev_sw, NF)
                pti, pei = divmod(pmp, 8)
                if (pg3, pti, pei) in rt_done:
                    Q.wait(cmv, rt_done[(pg3, pti, pei)])
            off = (gi * GTOK + ti * 128) * D
            Q.emit(lambda e, d=swt, s=sw_d[ei, off:off + 128 * D].rearrange(
                   "(t d) -> t d", d=D): e.dma_start(d[:], s), inc=dx, inc_n=16)
            swl = dx.n
            V.wait(dx, swl)
            rc_ = colp[:, 12:13]
            V.emit(lambda e, i0=xnt3, i1=swt, c=rc_:
                   e.scalar_tensor_tensor(junk[:, :], i0[:], 1.0, i1[:],
                                          ALU.mult, ALU.mult, accum_out=c),
                   inc=cmv)
            V.wait(cmv, cmv.n)
            V.emit(lambda e, c=rc_, b=b, ei=ei:
                   e.tensor_tensor(racc[:, b * 8 + ei:b * 8 + ei + 1],
                                   racc[:, b * 8 + ei:b * 8 + ei + 1], c, ALU.add),
                   inc=cmv)
            rt_done[(g3, ti, ei)] = cmv.n
            rt_v[(g3, ti)] = cmv.n

    p3_stack.close()
    p3cmt, p3cmv, p3cms, p3dsp = cmt.n, cmv.n, cms.n, dsp.n

    p4_stack = contextlib.ExitStack()
    sb4 = lambda name, shape, dt: p4_stack.enter_context(
        nc.sbuf_tensor(name, shape, dt))
    wdnsb = sb4("wdnsb", [128, NF * 1024], BF16)
    zsb4 = sb4("zsb4", [128, 2 * NF * 128], BF16)
    x2h = sb4("x2h", [128, 2 * 1024], F32)
    ssb4 = sb4("ssb4", [128, 2 * 1024], F32)
    psD = [p4_stack.enter_context(nc.psum_tensor(f"psD{j}", [128, 1024], F32))
           for j in range(2)]
    psR = p4_stack.enter_context(nc.psum_tensor("psR", [128, 16], F32))

    G.wait(cmt, p3cmt)
    G.wait(cmv, p3cmv)
    Q.wait(cmt, p3cmt)
    Q.wait(cmv, p3cmv)
    Q.wait(dsp, p3dsp)

    p4_free = {}
    for hf in range(2):
        if hf == 1:
            G.wait(cmt, cmt.n)
        for kb in range(8):
            src = wdn_d[kb * 1024:(kb + 1) * 1024,
                        hf * 1024:(hf + 1) * 1024].rearrange(
                "(k p) n -> p k n", p=128)
            dst = wdnsb[:, kb * 8 * 1024:(kb + 1) * 8 * 1024].rearrange(
                "p (k n) -> p k n", k=8)
            G.emit(lambda e, d=dst, s=src: e.dma_start(d, s), inc=dwdn, inc_n=16)
        wdn_ld = dwdn.n
        for b in range(B):
            for i in range(NT):
                t0 = i * TIL
                par = i % 2
                gi = i // GT
                ti = i % GT
                gidx4 = hf * 2 * NT + b * NT + i
                zt = zsb4[:, par * NF * 128:(par + 1) * NF * 128]
                x2t = x2h[:, par * 1024:(par + 1) * 1024]
                st_ = ssb4[:, par * 1024:(par + 1) * 1024]
                if gidx4 >= 2:
                    Q.wait(dsp, p4_free[gidx4 - 2][1])
                src = s_z[b, gi, :, :, ti * 128:(ti + 1) * 128].rearrange(
                    "f p t -> p f t")
                Q.emit(lambda e, d=zt.rearrange("p (f t) -> p f t", f=NF), s=src:
                       e.dma_start(d, s), inc=dx, inc_n=16)
                Q.emit(lambda e, d=x2t, s=s_x2[b, t0:t0 + TIL,
                                               hf * 1024:(hf + 1) * 1024]:
                       e.dma_start(d[:], s[:, :]), inc=dx, inc_n=16)
                ld4 = dx.n
                T.wait(dx, ld4)
                if gidx4 % (2 * NT) == 0:
                    T.wait(dwdn, wdn_ld)
                if gidx4 >= 2:
                    T.wait(cmv, p4_free[gidx4 - 2][0])
                ps = psD[par]
                for f in range(NF):
                    for nn in range(2):
                        T.emit(lambda e, ps=ps, f=f, nn=nn, zt=zt:
                               e.matmul(ps[:, nn * 512:(nn + 1) * 512],
                                        zt[:, f * 128:(f + 1) * 128],
                                        wdnsb[:, f * 1024 + nn * 512:
                                              f * 1024 + (nn + 1) * 512],
                                        start=(f == 0), stop=(f == NF - 1)),
                               inc=cmt if (f == NF - 1 and nn == 1) else None)
                V.wait(cmt, cmt.n)
                if gidx4 >= 2:
                    V.wait(dsp, p4_free[gidx4 - 2][1])
                V.emit(lambda e, o=st_, a=ps, bb=x2t:
                       e.tensor_tensor(o[:], a[:, :], bb[:], ALU.add), inc=cmv)
                sv = cmv.n
                SC.wait(cmv, sv)
                SC.emit(lambda e, s=st_, b=b, t0=t0, hf=hf:
                        e.dma_start(s_s[b, t0:t0 + TIL,
                                        hf * 1024:(hf + 1) * 1024], s[:]),
                        inc=dsp, inc_n=16)
                p4_free[gidx4] = (sv, dsp.n)

    for b in range(B):
        T.wait(cmv, p3cmv)
        T.emit(lambda e, b=b: e.matmul(psR[0:1, b * 8:b * 8 + 8],
                                       ones64[:, 0:1], racc[:, b * 8:b * 8 + 8],
                                       start=True, stop=True), inc=cmt)
        V.wait(cmt, cmt.n)
        V.emit(lambda e, b=b: e.tensor_copy(lgrow[0:1, b * 8:b * 8 + 8],
                                            psR[0:1, b * 8:b * 8 + 8]), inc=cmv)
        V.wait(cmv, cmv.n)
        V.emit(lambda e, b=b: e.tensor_tensor(lgrow[0:1, b * 8:b * 8 + 8],
                                              lgrow[0:1, b * 8:b * 8 + 8],
                                              sbrow[0:1, :], ALU.add), inc=cmv)
        T.wait(cmv, cmv.n)
        T.emit(lambda e, b=b: e.matmul(psR[:, b * 8:b * 8 + 8], ones1[:, :],
                                       lgrow[0:1, b * 8:b * 8 + 8],
                                       start=True, stop=True), inc=cmt)
        V.wait(cmt, cmt.n)
        lgs = lgb[:, b * 8:(b + 1) * 8]
        V.emit(lambda e, b=b, lgs=lgs: e.tensor_copy(lgs, psR[:, b * 8:b * 8 + 8]),
               inc=cmv)
        V.wait(cmv, cmv.n)
        c12 = colp[:, 12:13]
        c13 = colp[:, 13:14]
        c14 = colp[:, 14:15]
        def vsem(fn):
            V.emit(fn, inc=cmv)
            V.wait(cmv, cmv.n)
        vsem(lambda e, lgs=lgs, c=c12: e.reduce_max(c, lgs, AX.X))
        vsem(lambda e, c=c12: e.tensor_scalar_mul(c, c, -1.0))
        SC.wait(cmv, cmv.n)
        SC.emit(lambda e, lgs=lgs, c=c12: e.activation(lgs, lgs, AF.Exp, bias=c),
                inc=cms)
        V.wait(cms, cms.n)
        vsem(lambda e, lgs=lgs, c=c13: e.reduce_sum(c, lgs, AX.X))
        vsem(lambda e, c=c13: e.reciprocal(c, c))
        vsem(lambda e, lgs=lgs, c=c13: e.tensor_scalar_mul(lgs, lgs, c))
        for ei in range(8):
            vsem(lambda e, lgs=lgs, ei=ei:
                 e.tensor_single_scalar(tmp8[:, :], lgs, lgs[:, ei:ei + 1],
                                        ALU.is_gt))
            vsem(lambda e, c=c14: e.reduce_sum(c, tmp8[:, :], AX.X))
            vsem(lambda e, c=c14: e.tensor_single_scalar(c, c, 1.5, ALU.is_lt))
            vsem(lambda e, lgs=lgs, ei=ei, b=b, c=c14:
                 e.tensor_scalar_mul(gcol[:, b * 8 + ei:b * 8 + ei + 1],
                                     lgs[:, ei:ei + 1], c))
    gates_done = cmv.n
    if dbg:
        SC.wait(cmv, gates_done)
        SC.emit(lambda e: e.dma_start(d_gate[:, :], gcol[:, :]), inc=dsp, inc_n=16)

    p4_stack.close()
    p4cmt, p4cmv, p4cms, p4dsp = cmt.n, cmv.n, cms.n, dsp.n

    p5_stack = contextlib.ExitStack()
    sb5 = lambda name, shape, dt: p5_stack.enter_context(
        nc.sbuf_tensor(name, shape, dt))
    w3Tsb = sb5("w3Tsb", [128, NK * 2048], BF16)
    Hsb = sb5("Hsb", [128, 8 * 2048], BF16)
    xnT5 = sb5("xnT5", [128, 2 * 2048], BF16)
    xosb = sb5("xosb", [128, 2048], BF16)
    b3b = sb5("b3b", [128, D], F32)
    xs5 = sb5("xs5", [128, 2 * D], F32)
    ss5 = sb5("ss5", [128, 2 * D], F32)
    gsb = sb5("gsb", [128, D], F32)
    outsb = sb5("outsb", [128, 2 * D], F32)
    w1t = sb5("w1t", [128, 8 * 128], BF16)
    w2t = sb5("w2t", [128, 8 * 128], BF16)
    w2s = sb5("w2s", [128, 8 * 128], BF16)
    psH = [p5_stack.enter_context(nc.psum_tensor(f"psH{j}", [128, 512], F32))
           for j in range(2)]
    psO = [p5_stack.enter_context(nc.psum_tensor(f"psO{j}", [128, 512], F32))
           for j in range(2)]
    psW3 = p5_stack.enter_context(nc.psum_tensor("psW3", [128, 2048], F32))

    G.wait(cmt, p4cmt)
    G.wait(cmv, p4cmv)
    Q.wait(cmt, p4cmt)
    Q.wait(cmv, p4cmv)
    Q.wait(dsp, p4dsp)

    T.wait(cmv, p4cmv)
    for n in range(4):
        if n > 0:
            Q.wait(cmt, cmt.n)
        Q.emit(lambda e, n=n: e.dma_start(rowbuf[:, :],
                                          b3_d[None, 512 * n:512 * (n + 1)]),
               inc=dx, inc_n=16)
        T.wait(dx, dx.n)
        T.emit(lambda e, n=n: e.matmul(psW3[:, 512 * n:512 * (n + 1)],
                                       ones1[:, :], rowbuf[:, :],
                                       start=True, stop=True), inc=cmt)
    V.wait(cmt, cmt.n)
    V.emit(lambda e: e.tensor_copy(b3b[:, :], psW3[:, :]), inc=cmv)
    T.wait(cmv, cmv.n)

    Q.wait(dgp, 48)
    for ei in range(8):
        Q.emit(lambda e, ei=ei: e.dma_start(w1t[:, 128 * ei:128 * (ei + 1)],
                                            s_w1b[ei, :, :], transpose=True),
               inc=dtr, inc_n=16)
        Q.emit(lambda e, ei=ei: e.dma_start(w2t[:, 128 * ei:128 * (ei + 1)],
                                            s_w2b[ei, :, :], transpose=True),
               inc=dtr, inc_n=16)
    wexp_ld = dtr.n
    for k in range(NK):
        Q.emit(lambda e, k=k: e.dma_start(w3Tsb[:, k * 2048:(k + 1) * 2048],
                                          s_w3b[:, k * 128:(k + 1) * 128],
                                          transpose=True), inc=dtr, inc_n=16)
    w3t_ld = dtr.n

    p5_free = {}
    gelu_idx = 0
    gelu_cms = {}
    for b in range(B):
        V.wait(cmv, gates_done)
        V.wait(dtr, wexp_ld)
        for ei in range(8):
            V.emit(lambda e, ei=ei, b=b:
                   e.tensor_scalar_mul(w2s[:, ei * 128:(ei + 1) * 128],
                                       w2t[:, ei * 128:(ei + 1) * 128],
                                       gcol[:, b * 8 + ei:b * 8 + ei + 1]))
        V.emit(lambda e, b=b: e.tensor_tensor(tmp8[:, :], b2t[:, :],
                                              gcol[:, b * 8:(b + 1) * 8], ALU.mult),
               inc=cmv)
        V.wait(cmv, cmv.n)
        V.emit(lambda e, b=b: e.reduce_sum(b2mix[:, b:b + 1], tmp8[:, :], AX.X),
               inc=cmv)
        wscale_v = cmv.n
        for i in range(NT):
            t0 = i * TIL
            par = i % 2
            gidx5 = b * NT + i
            xnt5 = xnT5[:, par * 2048:(par + 1) * 2048]
            xt5 = xs5[:, par * D:(par + 1) * D]
            st5 = ss5[:, par * D:(par + 1) * D]
            ot5 = outsb[:, par * D:(par + 1) * D]
            if gidx5 >= 2:
                Q.wait(cmv, p5_free[gidx5 - 2][0])
                Q.wait(dout, p5_free[gidx5 - 2][1])
            for k in range(NK):
                Q.emit(lambda e, d=xnt5, k=k, b=b, t0=t0:
                       e.dma_start(d[:, k * 128:(k + 1) * 128],
                                   s_xnb[b, t0:t0 + TIL, k * 128:(k + 1) * 128],
                                   transpose=True), inc=dtr, inc_n=16)
            Q.emit(lambda e, d=xt5, b=b, t0=t0:
                   e.dma_start(d[:], x_d[b, t0:t0 + TIL, :]), inc=dx, inc_n=16)
            Q.emit(lambda e, d=st5, b=b, t0=t0:
                   e.dma_start(d[:], s_s[b, t0:t0 + TIL, :]), inc=dx, inc_n=16)
            ld5x = dx.n
            ld5t = dtr.n
            T.wait(dtr, ld5t)
            if gidx5 == 0:
                T.wait(dtr, w3t_ld)
            T.wait(cmv, wscale_v)
            for ei in range(8):
                for hg in range(4):
                    j = ei * 4 + hg
                    pj = j % 2
                    if gelu_idx + j >= 2:
                        T.wait(cms, gelu_cms.get(gelu_idx + j - 2, 0))
                    T.emit(lambda e, pj=pj, ei=ei, hg=hg, xnt5=xnt5:
                           e.matmul(psH[pj][:, :], w1t[:, ei * 128:(ei + 1) * 128],
                                    xnt5[:, hg * 512:(hg + 1) * 512],
                                    start=True, stop=True), inc=cmt)
                    mmv = cmt.n
                    SC.wait(cmt, mmv)
                    SC.emit(lambda e, pj=pj, ei=ei, hg=hg:
                            e.activation(Hsb[:, ei * 2048 + hg * 512:
                                             ei * 2048 + (hg + 1) * 512],
                                         psH[pj][:, :], AF.Gelu,
                                         bias=b1t[:, ei:ei + 1]), inc=cms)
                    gelu_cms[gelu_idx + j] = cms.n
            gelu_idx += 32
            T.wait(cms, cms.n)
            for hg in range(4):
                po = hg % 2
                if hg >= 2:
                    T.wait(cmv, cmv.n)
                for ei in range(8):
                    T.emit(lambda e, po=po, ei=ei, hg=hg:
                           e.matmul(psO[po][:, :], w2s[:, ei * 128:(ei + 1) * 128],
                                    Hsb[:, ei * 2048 + hg * 512:
                                        ei * 2048 + (hg + 1) * 512],
                                    start=(ei == 0), stop=(ei == 7)),
                           inc=cmt if ei == 7 else None)
                V.wait(cmt, cmt.n)
                V.emit(lambda e, po=po, hg=hg, xnt5=xnt5, b=b:
                       e.scalar_tensor_tensor(xosb[:, hg * 512:(hg + 1) * 512],
                                              psO[po][:, :], b2mix[:, b:b + 1],
                                              xnt5[:, hg * 512:(hg + 1) * 512],
                                              ALU.add, ALU.add), inc=cmv)
            xo_v = cmv.n
            T.wait(cmv, xo_v)
            for n in range(4):
                for k in range(NK):
                    T.emit(lambda e, n=n, k=k:
                           e.matmul(psW3[:, n * 512:(n + 1) * 512],
                                    xosb[:, k * 128:(k + 1) * 128],
                                    w3Tsb[:, k * 2048 + n * 512:
                                          k * 2048 + (n + 1) * 512],
                                    start=(k == 0), stop=(k == NK - 1)),
                           inc=cmt if (n == 3 and k == NK - 1) else None)
            V.wait(cmt, cmt.n)
            V.emit(lambda e: e.scalar_tensor_tensor(gsb[:, :], psW3[:, :], 1.0,
                                                    b3b[:, :], ALU.mult, ALU.add),
                   inc=cmv)
            gv = cmv.n
            SC.wait(cmv, gv)
            SC.emit(lambda e: e.activation(gsb[:, :], gsb[:, :], AF.Silu), inc=cms)
            V.wait(cms, cms.n)
            V.wait(dx, ld5x)
            if gidx5 >= 2:
                V.wait(dout, p5_free[gidx5 - 2][1])
            V.emit(lambda e, st5=st5: e.tensor_tensor(gsb[:, :], st5[:], gsb[:, :],
                                                      ALU.mult))
            V.emit(lambda e, ot5=ot5, xt5=xt5:
                   e.tensor_tensor(ot5[:], xt5[:], gsb[:, :], ALU.add), inc=cmv)
            ov = cmv.n
            SC.wait(cmv, ov)
            SC.emit(lambda e, ot5=ot5, b=b, t0=t0:
                    e.dma_start(out_d[b, t0:t0 + TIL, :], ot5[:]),
                    inc=dout, inc_n=16)
            p5_free[gidx5] = (ov, dout.n)

    Q.wait(dout, dout.n)

    p5_stack.close()
    return dict(nc=nc, stack=stack, streams=streams)


def finalize(ctx):
    nc = ctx["nc"]
    T, V, SC, Q, G = ctx["streams"]
    check_deadlock(ctx["streams"])
    with nc.Block() as block:
        @block.tensor
        def _(t):
            T.play(t)

        @block.vector
        def _(v):
            V.play(v)

        @block.scalar
        def _(s):
            SC.play(s)

        @block.sync
        def _(q):
            Q.play(q)

        @block.gpsimd
        def _(g):
            G.play(g)
    ctx["stack"].close()
    return nc


KEYMAP = dict(
    x="L0_attn_x", pw="L0_attn_pre_norm_w", wa="L0_attn_w_a", wi="L0_attn_w_i",
    wv="L0_attn_w_v", wop="L0_attn_out_proj_w", db="L0_attn_decay_bias",
    fw="L0_attn_ffn_norm_w", wgu="L0_attn_w_gate_up", wdn="L0_attn_w_down",
    sw="L0_ffn_switch_w", sb="L0_ffn_switch_b", w1="L0_ffn_w1", b1="L0_ffn_b1",
    w2="L0_ffn_w2", b2="L0_ffn_b2", w3="L0_ffn_w3", b3="L0_ffn_b3")


def kernel(**inputs):
    ctx = build(dbg=False)
    nc = finalize(ctx)
    full_x = np.ascontiguousarray(np.asarray(inputs["L0_attn_x"], np.float32))
    weights = {k: np.ascontiguousarray(np.asarray(inputs[v], np.float32))
               for k, v in KEYMAP.items() if k != "x"}
    weights["sw"] = weights["sw"].reshape(8, S * D)
    in_maps = []
    for c in range(8):
        m = dict(weights)
        m["x"] = full_x[2 * c:2 * c + 2]
        in_maps.append(m)
    res = run_bass_kernel_spmd(nc, in_maps, core_ids=list(range(8)))
    return np.concatenate([r["out"] for r in res.results], axis=0)


# version 05
# speedup vs baseline: 1.0192x; 1.0192x over previous
import contextlib
import numpy as np
import concourse.bass as bass
import concourse.mybir as mybir
from concourse.bass_utils import run_bass_kernel_spmd

F32 = mybir.dt.float32
BF16 = mybir.dt.bfloat16
AF = mybir.ActivationFunctionType
ALU = mybir.AluOpType
AX = mybir.AxisListType

CLAMP = 23.02585093
B = 2
S = 2048
D = 2048
TIL = 128
NT = S // TIL
GT = 8
NG = NT // GT
GTOK = GT * TIL
NK = D // 128
FF = 8192
NF = FF // 128
NCH = GTOK // 64


class Sem:
    def __init__(self, stack, nc, name):
        self.h = stack.enter_context(nc.semaphore(name))
        self.n = 0
        self.name = name


class Stream:
    def __init__(self, name):
        self.name = name
        self.ops = []

    def wait(self, sem, val):
        if val > 0:
            self.ops.append(("w", sem, val))

    def emit(self, fn, inc=None, inc_n=1):
        self.ops.append(("o", fn, inc, inc_n))
        if inc is not None:
            inc.n += inc_n

    def play(self, eng):
        for rec in self.ops:
            if rec[0] == "w":
                eng.wait_ge(rec[1].h, rec[2])
            else:
                inst = rec[1](eng)
                if rec[2] is not None:
                    inst.then_inc(rec[2].h, rec[3])


def check_deadlock(streams):
    cnt = {}
    idx = {st.name: 0 for st in streams}
    progress = True
    while progress:
        progress = False
        for st in streams:
            while idx[st.name] < len(st.ops):
                rec = st.ops[idx[st.name]]
                if rec[0] == "w":
                    if cnt.get(rec[1].name, 0) < rec[2]:
                        break
                else:
                    if rec[2] is not None:
                        cnt[rec[2].name] = cnt.get(rec[2].name, 0) + rec[3]
                idx[st.name] += 1
                progress = True
    stuck = {st.name: (idx[st.name], len(st.ops)) for st in streams
             if idx[st.name] < len(st.ops)}
    if stuck:
        details = []
        for st in streams:
            if idx[st.name] < len(st.ops):
                rec = st.ops[idx[st.name]]
                details.append(f"{st.name}@{idx[st.name]}/{len(st.ops)} waiting "
                               f"{rec[1].name}>={rec[2]} (have {cnt.get(rec[1].name, 0)})")
        raise RuntimeError("DEADLOCK: " + "; ".join(details))


def build(dbg=False):
    nc = bass.Bass("TRN2", target_bir_lowering=False)
    stack = contextlib.ExitStack()

    x_d = nc.dram_tensor("x", [B, S, D], F32, kind="ExternalInput")
    pw_d = nc.dram_tensor("pw", [D], F32, kind="ExternalInput")
    wa_d = nc.dram_tensor("wa", [D, D], F32, kind="ExternalInput")
    wi_d = nc.dram_tensor("wi", [D, D], F32, kind="ExternalInput")
    wv_d = nc.dram_tensor("wv", [D, D], F32, kind="ExternalInput")
    wop_d = nc.dram_tensor("wop", [D, D], F32, kind="ExternalInput")
    db_d = nc.dram_tensor("db", [D], F32, kind="ExternalInput")
    fw_d = nc.dram_tensor("fw", [D], F32, kind="ExternalInput")
    wgu_d = nc.dram_tensor("wgu", [D, 2 * FF], F32, kind="ExternalInput")
    wdn_d = nc.dram_tensor("wdn", [FF, D], F32, kind="ExternalInput")
    sw_d = nc.dram_tensor("sw", [8, S * D], F32, kind="ExternalInput")
    sb_d = nc.dram_tensor("sb", [8], F32, kind="ExternalInput")
    w1_d = nc.dram_tensor("w1", [8, 128, 128], F32, kind="ExternalInput")
    b1_d = nc.dram_tensor("b1", [8, 128], F32, kind="ExternalInput")
    w2_d = nc.dram_tensor("w2", [8, 128, 128], F32, kind="ExternalInput")
    b2_d = nc.dram_tensor("b2", [8, 128], F32, kind="ExternalInput")
    w3_d = nc.dram_tensor("w3", [D, D], F32, kind="ExternalInput")
    b3_d = nc.dram_tensor("b3", [D], F32, kind="ExternalInput")
    out_d = nc.dram_tensor("out", [B, S, D], F32, kind="ExternalOutput")

    ik = "ExternalOutput" if dbg else "Internal"
    s_xn = nc.dram_tensor("s_xn", [B, S, D], F32, kind=ik)
    s_xnb = nc.dram_tensor("s_xnb", [B, S, D], BF16, kind="Internal")
    s_h = nc.dram_tensor("s_h", [B, S, D], BF16, kind=ik)
    s_y = nc.dram_tensor("s_y", [B, NG, NK, 128, GTOK], BF16, kind=ik)
    s_x2 = nc.dram_tensor("s_x2", [B, S, D], F32, kind=ik)
    s_h2 = nc.dram_tensor("s_h2", [B, S, D], BF16, kind="Internal")
    s_z = nc.dram_tensor("s_z", [B, NG, NF, 128, GTOK], BF16, kind=ik)
    s_s = nc.dram_tensor("s_s", [B, S, D], F32, kind=ik)
    s_wt_aiv = [nc.dram_tensor(f"s_wt_aiv{j}", [128, NK * 2048], BF16,
                               kind="Internal") for j in range(3)]
    s_wt_g = nc.dram_tensor("s_wt_g", [128, NF * 2048], BF16, kind="Internal")
    s_wt_u = nc.dram_tensor("s_wt_u", [128, NF * 2048], BF16, kind="Internal")
    s_wt_dn = nc.dram_tensor("s_wt_dn", [4, 128, NF * 512], BF16, kind="Internal")
    s_w1b = nc.dram_tensor("s_w1b", [8, 128, 128], BF16, kind="Internal")
    s_w2b = nc.dram_tensor("s_w2b", [8, 128, 128], BF16, kind="Internal")
    s_w3b = nc.dram_tensor("s_w3b", [D, D], BF16, kind="Internal")
    d_gate = nc.dram_tensor("d_gate", [128, 8 * B], F32,
                            kind=ik) if dbg else None

    T = Stream("T")
    V = Stream("V")
    SC = Stream("SC")
    Q = Stream("Q")
    G = Stream("G")
    streams = [T, V, SC, Q, G]

    def mksem(name):
        return Sem(stack, nc, name)

    cmt = mksem("cmt")
    cmv = mksem("cmv")
    cms = mksem("cms")
    dx = mksem("dx")
    dsp = mksem("dsp")
    dgp = mksem("dgp")
    dtr = mksem("dtr")
    dwl = mksem("dwl")
    dtl3 = mksem("dtl3")
    dtl4 = mksem("dtl4")
    dwop = mksem("dwop")
    dpw = mksem("dpw")
    dps = [mksem(f"dps{i}") for i in range(2)]
    dout = mksem("dout")
    cpe = [mksem(f"cpe{i}") for i in range(4)]

    sbt = lambda name, shape, dt: stack.enter_context(nc.sbuf_tensor(name, shape, dt))
    ones1 = sbt("ones1", [1, 128], F32)
    ones64 = sbt("ones64", [128, 64], F32)
    ndb = sbt("ndb", [128, NK], F32)
    stc = sbt("stc", [128, NK], F32)
    pwb = sbt("pwb", [128, D], BF16)
    rowbuf = sbt("rowbuf", [1, 512], F32)
    racc = sbt("racc", [128, 8 * B], F32)
    junk = sbt("junk", [128, D], F32)
    colp = sbt("colp", [128, 16], F32)
    lgrow = sbt("lgrow", [1, 16], F32)
    sbrow = sbt("sbrow", [1, 8], F32)
    lgb = sbt("lgb", [128, 8 * B], F32)
    gcol = sbt("gcol", [128, 8 * B], F32)
    b1t = sbt("b1t", [128, 8], F32)
    b2t = sbt("b2t", [128, 8], F32)
    b2mix = sbt("b2mix", [128, B], F32)
    tmp8 = sbt("tmp8", [128, 8], F32)
    stg = sbt("stg", [128, 2 * 1024], F32)

    V.emit(lambda e: e.memset(ones1[:, :], 1.0))
    V.emit(lambda e: e.memset(ones64[:, :], 1.0))
    V.emit(lambda e: e.memset(racc[:, :], 0.0))
    V.emit(lambda e: e.memset(stc[:, :], 0.0), inc=cmv)

    Q.emit(lambda e: e.dma_start(ndb[:, :], db_d.rearrange("(k p) -> p k", p=128),
                                 allow_slow_non_contiguous=True),
           inc=dx, inc_n=16)
    for ei in range(8):
        Q.emit(lambda e, ei=ei: e.dma_start(b1t[:, ei:ei + 1], b1_d[ei, :, None],
                                            allow_slow_non_contiguous=True),
               inc=dx, inc_n=16)
        Q.emit(lambda e, ei=ei: e.dma_start(b2t[:, ei:ei + 1], b2_d[ei, :, None],
                                            allow_slow_non_contiguous=True),
               inc=dx, inc_n=16)
    Q.emit(lambda e: e.dma_start(sbrow[:, :], sb_d[None, :]), inc=dx, inc_n=16)
    V.wait(dx, 16)
    V.emit(lambda e: e.tensor_scalar_mul(ndb[:, :], ndb[:, :], -1.0), inc=cmv)

    with nc.psum_tensor("psb", [128, 512], F32) as psb:
        T.wait(cmv, cmv.n)
        for n in range(4):
            if n > 0:
                Q.wait(cmt, cmt.n)
            Q.emit(lambda e, n=n: e.dma_start(rowbuf[:, :],
                                              pw_d[None, 512 * n:512 * (n + 1)]),
                   inc=dx, inc_n=16)
            T.wait(dx, dx.n)
            T.wait(cmv, cmv.n)
            T.emit(lambda e, n=n: e.matmul(psb[:, :], ones1[:, :], rowbuf[:, :],
                                           start=True, stop=True), inc=cmt)
            V.wait(cmt, cmt.n)
            V.emit(lambda e, n=n: e.tensor_copy(pwb[:, 512 * n:512 * (n + 1)],
                                                psb[:, :]), inc=cmv)
    pwb_done = cmv.n

    G.emit(lambda e: e.dma_start(s_w1b[:, :, :], w1_d[:, :, :]), inc=dgp, inc_n=16)
    G.emit(lambda e: e.dma_start(s_w2b[:, :, :], w2_d[:, :, :]), inc=dgp, inc_n=16)
    G.emit(lambda e: e.dma_start(s_w3b[:, :], w3_d[:, :]), inc=dgp, inc_n=16)

    groups = [(b, gi) for b in range(B) for gi in range(NG)]
    xn_done = {}
    h_spill = {}
    hT_done = {}
    chain_mm = {}
    npool = [0, 0, 0, 0]
    wl_target = {}
    xnbb_sp = {}
    y_prev_sp = {}
    y_sp_last = {}
    unit_last_v = {}
    unit_last_s = {}
    wmats = [wa_d, wi_d, wv_d]

    p1 = {}
    p1_stack = contextlib.ExitStack()

    def p1_alloc():
        es = p1_stack.enter_context
        p1["hTg"] = es(nc.sbuf_tensor("hTg", [128, 2 * NK * GTOK], BF16))
        p1["ybuf"] = es(nc.sbuf_tensor("ybuf", [128, 2 * GTOK], BF16))
        p1["xsb"] = es(nc.sbuf_tensor("xsb", [128, 2 * D], F32))
        p1["hsb"] = es(nc.sbuf_tensor("hsb", [128, 2 * D], BF16))
        p1["xnbb"] = es(nc.sbuf_tensor("xnbb", [128, D], BF16))
        p1["wsb"] = es(nc.sbuf_tensor("wsb", [128, 6 * NK * 128], BF16))
        p1["sc"] = es(nc.sbuf_tensor("scp", [128, 12 * 1024], F32))
        for j in range(4):
            p1[f"pp{j}"] = es(nc.psum_tensor(f"pp{j}", [128, 1024], F32))

    p1_alloc()

    def stage_a(b, gi, i):
        t0 = gi * GTOK + i * TIL
        par = i % 2
        xt = p1["xsb"][:, par * D:(par + 1) * D]
        ht = p1["hsb"][:, par * D:(par + 1) * D]
        gidx = groups.index((b, gi))
        prev = None
        if i >= 2:
            prev = (b, gi, i - 2)
        elif gidx > 0:
            pb, pg = groups[gidx - 1]
            prev = (pb, pg, GT - 2 + i)
        if prev is not None:
            Q.wait(dsp, xn_done[prev][1])
        Q.emit(lambda e: e.dma_start(xt[:], x_d[b, t0:t0 + TIL, :]), inc=dx, inc_n=16)
        xld = dx.n
        SC.wait(dx, xld)
        c0 = colp[:, par * 5 + 0:par * 5 + 1]
        c2 = colp[:, par * 5 + 2:par * 5 + 3]
        c3 = colp[:, par * 5 + 3:par * 5 + 4]
        c4 = colp[:, par * 5 + 4:par * 5 + 5]
        SC.emit(lambda e: e.activation(junk[:, :], xt[:], AF.Copy, accum_out=c0),
                inc=cms)
        V.wait(cms, cms.n)
        V.emit(lambda e: e.tensor_scalar_mul(c0, c0, -1.0 / D), inc=cmv)
        V.wait(cmv, cmv.n)
        V.emit(lambda e: e.tensor_scalar_add(xt[:], xt[:], c0), inc=cmv)
        SC.wait(cmv, cmv.n)
        SC.emit(lambda e: e.activation(junk[:, :], xt[:], AF.Square, accum_out=c2),
                inc=cms)
        V.wait(cms, cms.n)
        V.emit(lambda e: e.tensor_scalar(c3, c2, 1.0 / D, 1e-5, ALU.mult, ALU.add),
               inc=cmv)
        SC.wait(cmv, cmv.n)
        SC.emit(lambda e: e.activation(c3, c3, AF.Ln), inc=cms)
        SC.wait(cms, cms.n)
        SC.emit(lambda e: e.activation(c3, c3, AF.Exp, scale=-0.5), inc=cms)
        V.wait(cms, cms.n)
        V.emit(lambda e: e.tensor_scalar(c4, c2, 1.0 / D, c3, ALU.mult, ALU.mult),
               inc=cmv)
        V.wait(cmv, cmv.n)
        V.emit(lambda e: e.tensor_scalar(c4, c4, c3, 1e-6, ALU.mult, ALU.add),
               inc=cmv)
        SC.wait(cmv, cmv.n)
        SC.emit(lambda e: e.activation(c4, c4, AF.Ln), inc=cms)
        SC.wait(cms, cms.n)
        SC.emit(lambda e: e.activation(c4, c4, AF.Exp, scale=-0.5), inc=cms)
        V.wait(cms, cms.n)
        V.emit(lambda e: e.tensor_scalar_mul(xt[:], xt[:], c3), inc=cmv)
        xnv = cmv.n
        if prev is not None:
            V.wait(dsp, xn_done[prev][1])
        V.emit(lambda e: e.scalar_tensor_tensor(ht[:], xt[:], c4, pwb[:, :],
                                                ALU.mult, ALU.mult), inc=cmv)
        hv = cmv.n
        SC.wait(cmv, hv)
        SC.emit(lambda e: e.dma_start(s_xn[b, t0:t0 + TIL, :], xt[:]),
                inc=dsp, inc_n=16)
        SC.emit(lambda e: e.dma_start(s_h[b, t0:t0 + TIL, :], ht[:]),
                inc=dsp, inc_n=16)
        gti = groups.index((b, gi)) * GT + i
        if gti > 0:
            V.wait(dsp, xnbb_sp[gti - 1])
        V.emit(lambda e: e.tensor_copy(p1["xnbb"][:, :], xt[:]), inc=cmv)
        SC.wait(cmv, cmv.n)
        SC.emit(lambda e, b=b, t0=t0: e.dma_start(s_xnb[b, t0:t0 + TIL, :],
                                                  p1["xnbb"][:, :]),
                inc=dsp, inc_n=16)
        xnbb_sp[gti] = dsp.n
        xn_sp = dsp.n
        xn_done[(b, gi, i)] = (xnv, xn_sp)

    def transposes(b, gi):
        gidx = groups.index((b, gi))
        gbuf = gidx % 2
        Q.wait(dsp, h_spill[(b, gi)])
        if gidx >= 2:
            pb, pg = groups[gidx - 2]
            Q.wait(cmt, chain_mm[(pb, pg, 3 * NK - 1)])
        for k in range(NK):
            dst = p1["hTg"][:, gbuf * NK * GTOK + k * GTOK:
                            gbuf * NK * GTOK + (k + 1) * GTOK]
            src = s_h[b, gi * GTOK:(gi + 1) * GTOK, k * 128:(k + 1) * 128]
            Q.emit(lambda e, d=dst, s=src: e.dma_start(d, s, transpose=True),
                   inc=dtr, inc_n=16)
        hT_done[(b, gi)] = dtr.n

    def chain_cmt(gci):
        return cmt_base + gci + 1

    p0w_units = []
    for m in range(NK):
        for wj in range(3):
            for h in range(2):
                srcap = wmats[wj][h * 1024:(h + 1) * 1024,
                                  m * 128:(m + 1) * 128].rearrange(
                    "(k p) c -> p k c", p=128)
                dst = s_wt_aiv[wj][:, m * 2048 + h * 1024:
                                   m * 2048 + h * 1024 + 1024].rearrange(
                    "p (k c) -> p k c", c=128)
                p0w_units.append((srcap, dst))
    N_AIV_U = len(p0w_units)
    for mp in range(NF):
        for half_t, base in ((s_wt_g, 0), (s_wt_u, FF)):
            for h in range(2):
                srcap = wgu_d[h * 1024:(h + 1) * 1024,
                              base + mp * 128:base + (mp + 1) * 128].rearrange(
                    "(k p) c -> p k c", p=128)
                dst = half_t[:, mp * 2048 + h * 1024:
                             mp * 2048 + h * 1024 + 1024].rearrange(
                    "p (k c) -> p k c", c=128)
                p0w_units.append((srcap, dst))
    N_GU_U = len(p0w_units) - N_AIV_U
    for q in range(4):
        for k2 in range(32):
            srcap = wdn_d[k2 * 256:(k2 + 1) * 256,
                          q * 512:(q + 1) * 512].rearrange(
                "(k p) n -> p k n", p=128)
            dst = s_wt_dn[q][:, k2 * 1024:(k2 + 1) * 1024].rearrange(
                "p (k n) -> p k n", n=512)
            p0w_units.append((srcap, dst))
    p0w_i = [0]

    def drain_p0w(n):
        for _ in range(n):
            j = p0w_i[0]
            if j >= len(p0w_units):
                return
            p0w_i[0] += 1
            srcap, dst = p0w_units[j]
            par = j % 2
            sg = stg[:, par * 1024:(par + 1) * 1024].rearrange(
                "p (k c) -> p k c", k=srcap.shape[1])
            if j >= 2:
                Q.wait(dps[j % 2], 16 * (j // 2))
            Q.emit(lambda e, d=sg, s=srcap: e.dma_start(d, s), inc=dpw, inc_n=16)
            G.wait(dpw, 16 * (j + 1))
            G.emit(lambda e, d=dst, s=sg: e.dma_start(d, s),
                   inc=dps[j % 2], inc_n=16)

    p1_tl = []
    for gidx0, (b0_, gi0_) in enumerate(groups):
        for ci in range(3 * NK):
            m, wj = divmod(ci, 3)
            gci = gidx0 * 3 * NK + ci
            buf = gci % 6
            dst = p1["wsb"][:, buf * NK * 128:(buf + 1) * NK * 128]
            srcap = s_wt_aiv[wj][:, m * 2048:(m + 1) * 2048]
            pu = 2 * (m * 3 + wj) + 2
            p1_tl.append((dst, srcap, gci, pu))
    p1_tl_i = [0]

    def drain_tl(n):
        for _ in range(n):
            j = p1_tl_i[0]
            if j >= len(p1_tl):
                return
            p1_tl_i[0] += 1
            dst, srcap, gci, pu = p1_tl[j]
            assert gci == j
            if gci < 3 * NK:
                Q.wait(dps[0], 16 * (N_AIV_U // 2))
                Q.wait(dps[1], 16 * (N_AIV_U // 2))
            if gci >= 6:
                Q.wait(cmt, chain_cmt(gci - 6))
            Q.emit(lambda e, d=dst, s=srcap: e.dma_start(d, s), inc=dwl, inc_n=16)

    def mm_chains(b, gi):
        gidx = groups.index((b, gi))
        gbuf = gidx % 2
        T.wait(dtr, hT_done[(b, gi)])
        for ci in range(3 * NK):
            m, wj = divmod(ci, 3)
            gci = gidx * 3 * NK + ci
            buf = gci % 6
            pool = ci % 4
            T.wait(dwl, 16 * (gci + 1))
            if npool[pool] > 0:
                T.wait(cpe[pool], npool[pool])
            ps = p1[f"pp{pool}"]
            wt = p1["wsb"][:, buf * NK * 128:(buf + 1) * NK * 128]
            for k in range(NK):
                for hh in range(2):
                    last = (k == NK - 1) and (hh == 1)
                    rhs = p1["hTg"][:, gbuf * NK * GTOK + k * GTOK + hh * 512:
                                    gbuf * NK * GTOK + k * GTOK + hh * 512 + 512]
                    T.emit(lambda e, ps=ps, w=wt[:, k * 128:(k + 1) * 128], r=rhs,
                           hh=hh, k=k:
                           e.matmul(ps[:, hh * 512:hh * 512 + 512], w, r,
                                    start=(k == 0), stop=(k == NK - 1)),
                           inc=cmt if last else None)
            chain_mm[(b, gi, ci)] = cmt.n
            assert cmt.n == chain_cmt(gci)
            npool[pool] += 1

    def sphase_a(b, gi, m):
        pool = m % 2
        bs = [p1["sc"][:, (pool * 6 + j) * 1024:(pool * 6 + j + 1) * 1024]
              for j in range(6)]
        prev = (b, gi, m - 2)
        if m < 2:
            gidx = groups.index((b, gi))
            if gidx > 0:
                pb, pg = groups[gidx - 1]
                prev = (pb, pg, NK - 2 + m)
            else:
                prev = None
        if prev is not None:
            SC.wait(cmv, unit_last_v[prev])
        SC.wait(cmt, chain_mm[(b, gi, 3 * m + 1)])
        pa = p1[f"pp{(3 * m) % 4}"]
        pi_ = p1[f"pp{(3 * m + 1) % 4}"]
        ndbc = ndb[:, m:m + 1]
        SC.emit(lambda e: e.activation(bs[0][:, :], pa[:, :], AF.Exp, bias=ndbc,
                                       scale=-1.0))
        SC.emit(lambda e: e.activation(bs[1][:, :], pi_[:, :], AF.Exp, scale=-1.0),
                inc=cms)
        SC.emit(lambda e: e.activation(colp[:, 10:11], colp[:, 10:11], AF.Copy),
                inc=cpe[(3 * m) % 4])
        SC.emit(lambda e: e.activation(colp[:, 10:11], colp[:, 10:11], AF.Copy),
                inc=cpe[(3 * m + 1) % 4])
        return cms.n

    def vphase_a(b, gi, m, sa):
        pool = m % 2
        bs = [p1["sc"][:, (pool * 6 + j) * 1024:(pool * 6 + j + 1) * 1024]
              for j in range(6)]
        prev = (b, gi, m - 2)
        if m < 2:
            gidx = groups.index((b, gi))
            if gidx > 0:
                pb, pg = groups[gidx - 1]
                prev = (pb, pg, NK - 2 + m)
            else:
                prev = None
        if prev is not None:
            V.wait(cms, unit_last_s[prev])
        V.wait(cms, sa)
        V.wait(cmt, chain_mm[(b, gi, 3 * m + 2)])
        pv = p1[f"pp{(3 * m + 2) % 4}"]
        V.emit(lambda e: e.tensor_copy(bs[2][:, :], pv[:, :]), inc=cpe[(3 * m + 2) % 4])
        V.emit(lambda e: e.tensor_scalar_add(bs[3][:, :], bs[0][:, :], 1.0))
        V.emit(lambda e: e.tensor_scalar_add(bs[4][:, :], bs[1][:, :], 1.0), inc=cmv)
        return cmv.n

    def sphase_b(b, gi, m, va):
        pool = m % 2
        bs = [p1["sc"][:, (pool * 6 + j) * 1024:(pool * 6 + j + 1) * 1024]
              for j in range(6)]
        SC.wait(cmv, va)
        SC.emit(lambda e: e.activation(bs[0][:, :], bs[3][:, :], AF.Ln))
        SC.emit(lambda e: e.activation(bs[1][:, :], bs[4][:, :], AF.Ln), inc=cms)
        return cms.n

    def vphase_b(b, gi, m, sb_):
        pool = m % 2
        bs = [p1["sc"][:, (pool * 6 + j) * 1024:(pool * 6 + j + 1) * 1024]
              for j in range(6)]
        V.wait(cms, sb_)
        V.emit(lambda e: e.tensor_scalar_min(bs[3][:, :], bs[0][:, :], CLAMP))
        V.emit(lambda e: e.memset(bs[0][:, :], 1.0))
        V.emit(lambda e: e.memset(
            bass.AP(bs[0].tensor, bs[0].offset + 64,
                    [bs[0].ap[0], [64, NCH - 1]]), 0.0), inc=cmv)
        V.wait(cmv, cmv.n)
        V.emit(lambda e: e.tensor_tensor_scan(bs[4][:, :], bs[0][:, :],
                                              bs[3][:, :], 0.0,
                                              ALU.mult, ALU.add))
        V.emit(lambda e: e.tensor_scalar_min(bs[5][:, :], bs[4][:, :], CLAMP),
               inc=cmv)
        return cmv.n

    def sphase_c(b, gi, m, vb):
        pool = m % 2
        bs = [p1["sc"][:, (pool * 6 + j) * 1024:(pool * 6 + j + 1) * 1024]
              for j in range(6)]
        SC.wait(cmv, vb)
        SC.emit(lambda e: e.activation(bs[0][:, :], bs[3][:, :], AF.Exp, scale=-1.0))
        SC.emit(lambda e: e.activation(bs[3][:, :], bs[4][:, :], AF.Exp, scale=-1.0))
        SC.emit(lambda e: e.activation(bs[5][:, :], bs[5][:, :], AF.Exp), inc=cms)
        return cms.n

    def vphase_c(b, gi, m, sc_):
        pool = m % 2
        bs = [p1["sc"][:, (pool * 6 + j) * 1024:(pool * 6 + j + 1) * 1024]
              for j in range(6)]
        V.wait(cms, sc_)
        V.emit(lambda e: e.tensor_tensor(bs[0][:, :], bs[0][:, :], bs[0][:, :],
                                         ALU.mult))
        V.emit(lambda e: e.tensor_scalar(bs[0][:, :], bs[0][:, :], -1.0, 1.0,
                                         ALU.mult, ALU.add))
        V.emit(lambda e: e.tensor_scalar_max(bs[0][:, :], bs[0][:, :], 1e-8),
               inc=cmv)
        return cmv.n

    def sphase_d(b, gi, m, vc):
        pool = m % 2
        bs = [p1["sc"][:, (pool * 6 + j) * 1024:(pool * 6 + j + 1) * 1024]
              for j in range(6)]
        SC.wait(cmv, vc)
        SC.emit(lambda e: e.activation(bs[0][:, :], bs[0][:, :], AF.Ln), inc=cms)
        return cms.n

    def vphase_d(b, gi, m, sd):
        pool = m % 2
        bs = [p1["sc"][:, (pool * 6 + j) * 1024:(pool * 6 + j + 1) * 1024]
              for j in range(6)]
        V.wait(cms, sd)
        V.emit(lambda e: e.scalar_tensor_tensor(bs[1][:, :], bs[0][:, :], 0.5,
                                                bs[1][:, :], ALU.mult, ALU.subtract),
               inc=cmv)
        return cmv.n

    def sphase_e(b, gi, m, vd):
        pool = m % 2
        bs = [p1["sc"][:, (pool * 6 + j) * 1024:(pool * 6 + j + 1) * 1024]
              for j in range(6)]
        SC.wait(cmv, vd)
        SC.emit(lambda e: e.activation(bs[1][:, :], bs[1][:, :], AF.Exp), inc=cms)
        unit_last_s[(b, gi, m)] = cms.n
        return cms.n

    def vphase_e(b, gi, m, se):
        pool = m % 2
        bs = [p1["sc"][:, (pool * 6 + j) * 1024:(pool * 6 + j + 1) * 1024]
              for j in range(6)]
        V.wait(cms, se)
        V.emit(lambda e: e.tensor_tensor(bs[2][:, :], bs[1][:, :], bs[2][:, :],
                                         ALU.mult))
        V.emit(lambda e: e.tensor_tensor(bs[2][:, :], bs[2][:, :], bs[5][:, :],
                                         ALU.mult))
        stcol = stc[:, m:m + 1]
        V.emit(lambda e: e.memset(bs[0][:, :], 1.0), inc=cmv)
        V.wait(cmv, cmv.n)
        V.emit(lambda e: e.tensor_copy(
            bass.AP(bs[0].tensor, bs[0].offset + 64, [bs[0].ap[0], [64, NCH - 1]]),
            bass.AP(bs[3].tensor, bs[3].offset + 63, [bs[3].ap[0], [64, NCH - 1]])),
            inc=cmv)
        V.wait(cmv, cmv.n)
        V.emit(lambda e: e.tensor_tensor_scan(bs[4][:, :], bs[0][:, :],
                                              bs[2][:, :], stcol,
                                              ALU.mult, ALU.add))
        ydst = p1["ybuf"][:, (m % 2) * GTOK:(m % 2 + 1) * GTOK]
        if (b, gi, m) in y_prev_sp:
            V.wait(dsp, y_prev_sp[(b, gi, m)])
        V.emit(lambda e, ydst=ydst: e.tensor_tensor(ydst, bs[3][:, :], bs[4][:, :],
                                                    ALU.mult))
        V.emit(lambda e: e.tensor_tensor(stcol, bs[3][:, 1023:1024],
                                         bs[4][:, 1023:1024], ALU.mult), inc=cmv)
        unit_last_v[(b, gi, m)] = cmv.n
        SC.wait(cmv, cmv.n)
        SC.emit(lambda e, b=b, gi=gi, m=m, ydst=ydst: e.dma_start(
            s_y[b, gi, m], ydst), inc=dsp, inc_n=16)
        nkey = None
        if m + 2 < NK:
            nkey = (b, gi, m + 2)
        else:
            gg = groups.index((b, gi))
            if gg + 1 < len(groups):
                nb_, ngi_ = groups[gg + 1]
                nkey = (nb_, ngi_, m + 2 - NK)
        if nkey is not None:
            y_prev_sp[nkey] = dsp.n
        y_sp_last[(b, gi)] = dsp.n
        return cmv.n

    def scan_pair(b, gi, m0):
        ms = [m0, m0 + 1]
        sa = [sphase_a(b, gi, m) for m in ms]
        va = [vphase_a(b, gi, m, sa[j]) for j, m in enumerate(ms)]
        sb_ = [sphase_b(b, gi, m, va[j]) for j, m in enumerate(ms)]
        vb = [vphase_b(b, gi, m, sb_[j]) for j, m in enumerate(ms)]
        sc_ = [sphase_c(b, gi, m, vb[j]) for j, m in enumerate(ms)]
        vc = [vphase_c(b, gi, m, sc_[j]) for j, m in enumerate(ms)]
        sd = [sphase_d(b, gi, m, vc[j]) for j, m in enumerate(ms)]
        vd = [vphase_d(b, gi, m, sd[j]) for j, m in enumerate(ms)]
        se = [sphase_e(b, gi, m, vd[j]) for j, m in enumerate(ms)]
        for j, m in enumerate(ms):
            vphase_e(b, gi, m, se[j])

    cmt_base = cmt.n
    y_spill = {}
    for gidx, (b, gi) in enumerate(groups):
        if gidx == 0:
            V.wait(cmv, pwb_done)
            T.wait(cmv, pwb_done)
            for i in range(GT):
                stage_a(b, gi, i)
            h_spill[(b, gi)] = dsp.n
            transposes(b, gi)
            drain_p0w(N_AIV_U)
            drain_tl(10)
        if gi == 0 and b > 0:
            V.emit(lambda e: e.memset(stc[:, :], 0.0), inc=cmv)
        mm_chains(b, gi)
        if gidx + 1 < len(groups):
            nb, ngi = groups[gidx + 1]
        else:
            nb = None
        for mp in range(NK // 2):
            scan_pair(b, gi, 2 * mp)
            if nb is not None and mp < GT // 2:
                stage_a(nb, ngi, 2 * mp)
                stage_a(nb, ngi, 2 * mp + 1)
                if mp == GT // 2 - 1:
                    h_spill[(nb, ngi)] = dsp.n
                    transposes(nb, ngi)
            drain_tl(6)
            drain_p0w(8)
        y_spill[(b, gi)] = y_sp_last[(b, gi)]
    assert p1_tl_i[0] == len(p1_tl), (p1_tl_i, len(p1_tl))

    p1_stack.close()
    p1cmt, p1cmv, p1cms, p1dsp = cmt.n, cmv.n, cms.n, dsp.n

    p2_stack = contextlib.ExitStack()
    sb2 = lambda name, shape, dt: p2_stack.enter_context(
        nc.sbuf_tensor(name, shape, dt))
    wopsb = sb2("wopsb", [128, NK * 2048], BF16)
    fwb = sb2("fwb", [128, D], F32)
    ysb = sb2("ysb", [128, 2 * 2048], BF16)
    xnb2 = sb2("xnb2", [128, 2 * D], F32)
    h2b = sb2("h2b", [128, 2 * D], BF16)
    psW2 = [p2_stack.enter_context(nc.psum_tensor(f"psW2{j}", [128, 2048], F32))
            for j in range(2)]

    G.wait(cmt, p1cmt)
    G.wait(cmv, p1cmv)
    Q.wait(cmt, p1cmt)
    Q.wait(cmv, p1cmv)
    Q.wait(dsp, p1dsp)

    for kb in range(4):
        src = wop_d[kb * 512:(kb + 1) * 512, :].rearrange("(k p) n -> p k n", p=128)
        dst = wopsb[:, kb * 4 * 2048:(kb + 1) * 4 * 2048].rearrange(
            "p (k n) -> p k n", k=4)
        G.emit(lambda e, d=dst, s=src: e.dma_start(d, s), inc=dwop, inc_n=16)
    wop_ld = dwop.n

    T.wait(cmv, p1cmv)
    for n in range(4):
        if n > 0:
            Q.wait(cmt, cmt.n)
        Q.emit(lambda e, n=n: e.dma_start(rowbuf[:, :],
                                          fw_d[None, 512 * n:512 * (n + 1)]),
               inc=dx, inc_n=16)
        T.wait(dx, dx.n)
        T.emit(lambda e, n=n: e.matmul(psW2[0][:, 512 * n:512 * (n + 1)],
                                       ones1[:, :], rowbuf[:, :],
                                       start=True, stop=True), inc=cmt)
    V.wait(cmt, cmt.n)
    V.emit(lambda e: e.tensor_copy(fwb[:, :], psW2[0][:, :]), inc=cmv)
    fwb_done = cmv.n
    T.wait(cmv, fwb_done)

    p2_free = {}
    for b in range(B):
        for i in range(NT):
            t0 = i * TIL
            par = i % 2
            gi = i // GT
            ti = i % GT
            gidx2 = b * NT + i
            yt = ysb[:, par * 2048:(par + 1) * 2048]
            xnt = xnb2[:, par * D:(par + 1) * D]
            h2t = h2b[:, par * D:(par + 1) * D]
            if gidx2 >= 2:
                Q.wait(dsp, p2_free[gidx2 - 2][1])
            Q.wait(dsp, y_spill[(b, gi)])
            src = s_y[b, gi, :, :, ti * 128:(ti + 1) * 128].rearrange(
                "k p t -> p k t")
            Q.emit(lambda e, d=yt.rearrange("p (k t) -> p k t", k=NK), s=src:
                   e.dma_start(d, s), inc=dx, inc_n=16)
            Q.emit(lambda e, d=xnt, s=s_xn[b, t0:t0 + TIL, :]:
                   e.dma_start(d, s[:, :]), inc=dx, inc_n=16)
            ld = dx.n
            T.wait(dx, ld)
            if gidx2 == 0:
                T.wait(dwop, wop_ld)
            if gidx2 >= 2:
                T.wait(cmv, p2_free[gidx2 - 2][0])
            ps = psW2[par]
            for n in range(4):
                for k in range(NK):
                    T.emit(lambda e, ps=ps, n=n, k=k, yt=yt:
                           e.matmul(ps[:, n * 512:(n + 1) * 512],
                                    yt[:, k * 128:(k + 1) * 128],
                                    wopsb[:, k * 2048 + n * 512:
                                          k * 2048 + (n + 1) * 512],
                                    start=(k == 0), stop=(k == NK - 1)),
                           inc=cmt if (n == 3 and k == NK - 1) else None)
            V.wait(cmt, cmt.n)
            V.emit(lambda e, ps=ps, xnt=xnt:
                   e.tensor_tensor(xnt[:], ps[:, :], xnt[:], ALU.add), inc=cmv)
            x2v = cmv.n
            c5 = colp[:, 10 + par:11 + par]
            SC.wait(cmv, x2v)
            SC.emit(lambda e, xnt=xnt, c=c5:
                    e.activation(junk[:, :], xnt[:], AF.Square, accum_out=c),
                    inc=cms)
            V.wait(cms, cms.n)
            V.emit(lambda e, c=c5: e.tensor_scalar(c, c, 1.0 / D, 1e-6,
                                                   ALU.mult, ALU.add), inc=cmv)
            SC.wait(cmv, cmv.n)
            SC.emit(lambda e, c=c5: e.activation(c, c, AF.Ln), inc=cms)
            SC.wait(cms, cms.n)
            SC.emit(lambda e, c=c5: e.activation(c, c, AF.Exp, scale=-0.5), inc=cms)
            V.wait(cms, cms.n)
            if gidx2 >= 2:
                V.wait(dsp, p2_free[gidx2 - 2][1])
            V.emit(lambda e, o=h2t, i_=xnt, c=c5:
                   e.scalar_tensor_tensor(o[:], i_[:], c, fwb[:, :],
                                          ALU.mult, ALU.mult), inc=cmv)
            h2v = cmv.n
            SC.wait(cmv, h2v)
            SC.emit(lambda e, s=xnt, b=b, t0=t0:
                    e.dma_start(s_x2[b, t0:t0 + TIL, :], s[:]), inc=dsp, inc_n=16)
            SC.emit(lambda e, s=h2t, b=b, t0=t0:
                    e.dma_start(s_h2[b, t0:t0 + TIL, :], s[:]), inc=dsp, inc_n=16)
            p2_free[gidx2] = (x2v, dsp.n)
            drain_p0w(8)

    p2_stack.close()
    p2cmt, p2cmv, p2cms, p2dsp = cmt.n, cmv.n, cms.n, dsp.n

    p3_stack = contextlib.ExitStack()
    sb3 = lambda name, shape, dt: p3_stack.enter_context(
        nc.sbuf_tensor(name, shape, dt))
    h2Tg = sb3("h2Tg", [128, 2 * NK * GTOK], BF16)
    wgsb = sb3("wgsb", [128, 2 * NK * 128], BF16)
    wusb = sb3("wusb", [128, 2 * NK * 128], BF16)
    zgsb = sb3("zgsb", [128, 2 * GTOK], F32)
    zbsb = sb3("zbsb", [128, 2 * GTOK], BF16)
    swsb = sb3("swsb", [128, 2 * D], F32)
    xn3 = sb3("xn3", [128, 2 * D], F32)
    psG = [p3_stack.enter_context(nc.psum_tensor(f"psG{j}", [128, GTOK], F32))
           for j in range(2)]
    psU = [p3_stack.enter_context(nc.psum_tensor(f"psU{j}", [128, GTOK], F32))
           for j in range(2)]

    G.wait(cmt, p2cmt)
    G.wait(cmv, p2cmv)
    Q.wait(cmt, p2cmt)
    Q.wait(cmv, p2cmv)
    Q.wait(dsp, p2dsp)

    p3_mp_done = {}
    p3_mp_free = {}
    p3_groups = [(b, gi) for b in range(B) for gi in range(NG)]
    h2T_done3 = {}
    zsp = {}
    rt_v = {}
    rt_done = {}

    p3_tl_done = {}

    def p3_load(g3, mp):
        b_, gi_ = p3_groups[g3]
        gmp = g3 * NF + mp
        par = mp % 2
        if gmp >= 2:
            Q.wait(cmt, p3_mp_done[divmod(gmp - 2, NF)])
        if g3 == 0:
            Q.wait(dps[0], 16 * (len(p0w_units) // 2))
            Q.wait(dps[1], 16 * (len(p0w_units) // 2))
        wg_dst = wgsb[:, par * NK * 128:(par + 1) * NK * 128]
        wu_dst = wusb[:, par * NK * 128:(par + 1) * NK * 128]
        Q.emit(lambda e, d=wg_dst, s=s_wt_g[:, mp * 2048:(mp + 1) * 2048]:
               e.dma_start(d, s), inc=dtl3, inc_n=16)
        Q.emit(lambda e, d=wu_dst, s=s_wt_u[:, mp * 2048:(mp + 1) * 2048]:
               e.dma_start(d, s), inc=dtl3, inc_n=16)
        p3_tl_done[(g3, mp)] = dtl3.n

    for g3, (b, gi) in enumerate(p3_groups):
        gbuf = g3 % 2
        if g3 >= 2:
            Q.wait(cmt, p3_mp_done[(g3 - 2, NF - 1)])
        for k in range(NK):
            dst = h2Tg[:, gbuf * NK * GTOK + k * GTOK:
                       gbuf * NK * GTOK + (k + 1) * GTOK]
            src = s_h2[b, gi * GTOK:(gi + 1) * GTOK, k * 128:(k + 1) * 128]
            Q.emit(lambda e, d=dst, s=src: e.dma_start(d, s, transpose=True),
                   inc=dtr, inc_n=16)
        h2T_done3[g3] = dtr.n
        p3_load(g3, 0)
        p3_load(g3, 1)
        for mp in range(NF):
            par = mp % 2
            gmp = g3 * NF + mp
            wg_dst = wgsb[:, par * NK * 128:(par + 1) * NK * 128]
            wu_dst = wusb[:, par * NK * 128:(par + 1) * NK * 128]
            T.wait(dtl3, p3_tl_done[(g3, mp)])
            if mp == 0:
                T.wait(dtr, h2T_done3[g3])
            if gmp >= 2:
                pg3, pmp = divmod(gmp - 2, NF)
                T.wait(cms, p3_mp_free[(pg3, pmp)][0])
                T.wait(cmv, p3_mp_free[(pg3, pmp)][1])
            for k in range(NK):
                for hh in range(2):
                    rhs = h2Tg[:, gbuf * NK * GTOK + k * GTOK + hh * 512:
                               gbuf * NK * GTOK + k * GTOK + hh * 512 + 512]
                    T.emit(lambda e, ps=psG[par], w=wg_dst[:, k * 128:(k + 1) * 128],
                           r=rhs, hh=hh, k=k:
                           e.matmul(ps[:, hh * 512:hh * 512 + 512], w, r,
                                    start=(k == 0), stop=(k == NK - 1)),
                           inc=cmt if (k == NK - 1 and hh == 1) else None)
            gate_mm = cmt.n
            for k in range(NK):
                for hh in range(2):
                    rhs = h2Tg[:, gbuf * NK * GTOK + k * GTOK + hh * 512:
                               gbuf * NK * GTOK + k * GTOK + hh * 512 + 512]
                    T.emit(lambda e, ps=psU[par], w=wu_dst[:, k * 128:(k + 1) * 128],
                           r=rhs, hh=hh, k=k:
                           e.matmul(ps[:, hh * 512:hh * 512 + 512], w, r,
                                    start=(k == 0), stop=(k == NK - 1)),
                           inc=cmt if (k == NK - 1 and hh == 1) else None)
            p3_mp_done[(g3, mp)] = cmt.n
            zg = zgsb[:, par * GTOK:(par + 1) * GTOK]
            zb = zbsb[:, par * GTOK:(par + 1) * GTOK]
            SC.wait(cmt, gate_mm)
            SC.emit(lambda e, o=zg, i_=psG[par]: e.activation(o[:], i_[:, :], AF.Silu),
                    inc=cms)
            silu_v = cms.n
            V.wait(cmt, p3_mp_done[(g3, mp)])
            V.wait(cms, silu_v)
            if gmp >= 2:
                V.wait(dsp, zsp[divmod(gmp - 2, NF)])
            V.emit(lambda e, o=zb, a=zg, bb=psU[par]:
                   e.tensor_tensor(o[:], a[:], bb[:, :], ALU.mult), inc=cmv)
            mult_v = cmv.n
            p3_mp_free[(g3, mp)] = (silu_v, mult_v)
            SC.wait(cmv, mult_v)
            SC.emit(lambda e, s=zb, b=b, gi=gi, mp=mp:
                    e.dma_start(s_z[b, gi, mp], s[:]), inc=dsp, inc_n=16)
            zsp[(g3, mp)] = dsp.n
            ti, ei = divmod(mp, 8)
            rpar = ti % 2
            xnt3 = xn3[:, rpar * D:(rpar + 1) * D]
            swt = swsb[:, par * D:(par + 1) * D]
            if ei == 0:
                prev_rt = (g3, ti - 2) if ti >= 2 else ((g3 - 1, ti + GT - 2)
                                                        if g3 > 0 else None)
                if prev_rt is not None and prev_rt in rt_v:
                    Q.wait(cmv, rt_v[prev_rt])
                Q.emit(lambda e, d=xnt3, s=s_xn[b, gi * GTOK + ti * 128:
                                                gi * GTOK + (ti + 1) * 128, :]:
                       e.dma_start(d[:], s[:, :]), inc=dx, inc_n=16)
            prev_sw = gmp - 2
            if prev_sw >= 0:
                pg3, pmp = divmod(prev_sw, NF)
                pti, pei = divmod(pmp, 8)
                if (pg3, pti, pei) in rt_done:
                    Q.wait(cmv, rt_done[(pg3, pti, pei)])
            off = (gi * GTOK + ti * 128) * D
            Q.emit(lambda e, d=swt, s=sw_d[ei, off:off + 128 * D].rearrange(
                   "(t d) -> t d", d=D): e.dma_start(d[:], s), inc=dx, inc_n=16)
            swl = dx.n
            if mp + 2 < NF:
                p3_load(g3, mp + 2)
            drain_p0w(1)
            V.wait(dx, swl)
            rc_ = colp[:, 12:13]
            V.emit(lambda e, i0=xnt3, i1=swt, c=rc_:
                   e.scalar_tensor_tensor(junk[:, :], i0[:], 1.0, i1[:],
                                          ALU.mult, ALU.mult, accum_out=c),
                   inc=cmv)
            V.wait(cmv, cmv.n)
            V.emit(lambda e, c=rc_, b=b, ei=ei:
                   e.tensor_tensor(racc[:, b * 8 + ei:b * 8 + ei + 1],
                                   racc[:, b * 8 + ei:b * 8 + ei + 1], c, ALU.add),
                   inc=cmv)
            rt_done[(g3, ti, ei)] = cmv.n
            rt_v[(g3, ti)] = cmv.n

    p3_stack.close()
    p3cmt, p3cmv, p3cms, p3dsp = cmt.n, cmv.n, cms.n, dsp.n

    p4_stack = contextlib.ExitStack()
    sb4 = lambda name, shape, dt: p4_stack.enter_context(
        nc.sbuf_tensor(name, shape, dt))
    wdnsb = sb4("wdnsb", [128, 2 * NF * 512], BF16)
    zsb4 = sb4("zsb4", [128, 2 * NF * 128], BF16)
    x2h = sb4("x2h", [128, 2 * 512], F32)
    ssb4 = sb4("ssb4", [128, 2 * 512], F32)
    psD = [p4_stack.enter_context(nc.psum_tensor(f"psD{j}", [128, 512], F32))
           for j in range(2)]
    psR = p4_stack.enter_context(nc.psum_tensor("psR", [128, 16], F32))

    G.wait(cmt, p3cmt)
    G.wait(cmv, p3cmv)
    Q.wait(cmt, p3cmt)
    Q.wait(cmv, p3cmv)
    Q.wait(dsp, p3dsp)

    p4_free = {}
    q_done = {}
    for q in range(4):
        qpar = q % 2
        if q >= 2:
            Q.wait(cmt, q_done[q - 2])
        Q.wait(dps[0], 16 * (len(p0w_units) // 2))
        Q.wait(dps[1], 16 * (len(p0w_units) // 2))
        for kb in range(4):
            Q.emit(lambda e, d=wdnsb[:, qpar * NF * 512 + kb * 8192:
                                     qpar * NF * 512 + (kb + 1) * 8192],
                   s=s_wt_dn[q][:, kb * 8192:(kb + 1) * 8192]:
                   e.dma_start(d, s), inc=dtl4, inc_n=16)
        wdn_t = dtl4.n
        for b in range(B):
            for i in range(NT):
                t0 = i * TIL
                par = i % 2
                gi = i // GT
                ti = i % GT
                gidx4 = q * 2 * NT + b * NT + i
                zt = zsb4[:, par * NF * 128:(par + 1) * NF * 128]
                x2t = x2h[:, par * 512:(par + 1) * 512]
                st_ = ssb4[:, par * 512:(par + 1) * 512]
                if gidx4 >= 2:
                    Q.wait(dsp, p4_free[gidx4 - 2][1])
                src = s_z[b, gi, :, :, ti * 128:(ti + 1) * 128].rearrange(
                    "f p t -> p f t")
                Q.emit(lambda e, d=zt.rearrange("p (f t) -> p f t", f=NF), s=src:
                       e.dma_start(d, s), inc=dx, inc_n=16)
                Q.emit(lambda e, d=x2t, s=s_x2[b, t0:t0 + TIL,
                                               q * 512:(q + 1) * 512]:
                       e.dma_start(d[:], s[:, :]), inc=dx, inc_n=16)
                ld4 = dx.n
                T.wait(dx, ld4)
                if gidx4 % (2 * NT) == 0:
                    T.wait(dtl4, wdn_t)
                if gidx4 >= 2:
                    T.wait(cmv, p4_free[gidx4 - 2][0])
                ps = psD[par]
                for f in range(NF):
                    T.emit(lambda e, ps=ps, f=f, zt=zt, qpar=qpar:
                           e.matmul(ps[:, :], zt[:, f * 128:(f + 1) * 128],
                                    wdnsb[:, qpar * NF * 512 + f * 512:
                                          qpar * NF * 512 + (f + 1) * 512],
                                    start=(f == 0), stop=(f == NF - 1)),
                           inc=cmt if f == NF - 1 else None)
                V.wait(cmt, cmt.n)
                if gidx4 >= 2:
                    V.wait(dsp, p4_free[gidx4 - 2][1])
                V.emit(lambda e, o=st_, a=ps, bb=x2t:
                       e.tensor_tensor(o[:], a[:, :], bb[:], ALU.add), inc=cmv)
                sv = cmv.n
                SC.wait(cmv, sv)
                SC.emit(lambda e, s=st_, b=b, t0=t0, q=q:
                        e.dma_start(s_s[b, t0:t0 + TIL,
                                        q * 512:(q + 1) * 512], s[:]),
                        inc=dsp, inc_n=16)
                p4_free[gidx4] = (sv, dsp.n)
        q_done[q] = cmt.n

    for b in range(B):
        T.wait(cmv, p3cmv)
        T.emit(lambda e, b=b: e.matmul(psR[0:1, b * 8:b * 8 + 8],
                                       ones64[:, 0:1], racc[:, b * 8:b * 8 + 8],
                                       start=True, stop=True), inc=cmt)
        V.wait(cmt, cmt.n)
        V.emit(lambda e, b=b: e.tensor_copy(lgrow[0:1, b * 8:b * 8 + 8],
                                            psR[0:1, b * 8:b * 8 + 8]), inc=cmv)
        V.wait(cmv, cmv.n)
        V.emit(lambda e, b=b: e.tensor_tensor(lgrow[0:1, b * 8:b * 8 + 8],
                                              lgrow[0:1, b * 8:b * 8 + 8],
                                              sbrow[0:1, :], ALU.add), inc=cmv)
        T.wait(cmv, cmv.n)
        T.emit(lambda e, b=b: e.matmul(psR[:, b * 8:b * 8 + 8], ones1[:, :],
                                       lgrow[0:1, b * 8:b * 8 + 8],
                                       start=True, stop=True), inc=cmt)
        V.wait(cmt, cmt.n)
        lgs = lgb[:, b * 8:(b + 1) * 8]
        V.emit(lambda e, b=b, lgs=lgs: e.tensor_copy(lgs, psR[:, b * 8:b * 8 + 8]),
               inc=cmv)
        V.wait(cmv, cmv.n)
        c12 = colp[:, 12:13]
        c13 = colp[:, 13:14]
        c14 = colp[:, 14:15]
        def vsem(fn):
            V.emit(fn, inc=cmv)
            V.wait(cmv, cmv.n)
        vsem(lambda e, lgs=lgs, c=c12: e.reduce_max(c, lgs, AX.X))
        vsem(lambda e, c=c12: e.tensor_scalar_mul(c, c, -1.0))
        SC.wait(cmv, cmv.n)
        SC.emit(lambda e, lgs=lgs, c=c12: e.activation(lgs, lgs, AF.Exp, bias=c),
                inc=cms)
        V.wait(cms, cms.n)
        vsem(lambda e, lgs=lgs, c=c13: e.reduce_sum(c, lgs, AX.X))
        vsem(lambda e, c=c13: e.reciprocal(c, c))
        vsem(lambda e, lgs=lgs, c=c13: e.tensor_scalar_mul(lgs, lgs, c))
        for ei in range(8):
            vsem(lambda e, lgs=lgs, ei=ei:
                 e.tensor_single_scalar(tmp8[:, :], lgs, lgs[:, ei:ei + 1],
                                        ALU.is_gt))
            vsem(lambda e, c=c14: e.reduce_sum(c, tmp8[:, :], AX.X))
            vsem(lambda e, c=c14: e.tensor_single_scalar(c, c, 1.5, ALU.is_lt))
            vsem(lambda e, lgs=lgs, ei=ei, b=b, c=c14:
                 e.tensor_scalar_mul(gcol[:, b * 8 + ei:b * 8 + ei + 1],
                                     lgs[:, ei:ei + 1], c))
    gates_done = cmv.n
    if dbg:
        SC.wait(cmv, gates_done)
        SC.emit(lambda e: e.dma_start(d_gate[:, :], gcol[:, :]), inc=dsp, inc_n=16)

    p4_stack.close()
    p4cmt, p4cmv, p4cms, p4dsp = cmt.n, cmv.n, cms.n, dsp.n

    p5_stack = contextlib.ExitStack()
    sb5 = lambda name, shape, dt: p5_stack.enter_context(
        nc.sbuf_tensor(name, shape, dt))
    w3Tsb = sb5("w3Tsb", [128, NK * 2048], BF16)
    Hsb = sb5("Hsb", [128, 8 * 2048], BF16)
    xnT5 = sb5("xnT5", [128, 2 * 2048], BF16)
    xosb = sb5("xosb", [128, 2048], BF16)
    b3b = sb5("b3b", [128, D], F32)
    xs5 = sb5("xs5", [128, 2 * D], F32)
    ss5 = sb5("ss5", [128, 2 * D], F32)
    gsb = sb5("gsb", [128, D], F32)
    outsb = sb5("outsb", [128, 2 * D], F32)
    w1t = sb5("w1t", [128, 8 * 128], BF16)
    w2t = sb5("w2t", [128, 8 * 128], BF16)
    w2s = sb5("w2s", [128, 8 * 128], BF16)
    psH = [p5_stack.enter_context(nc.psum_tensor(f"psH{j}", [128, 512], F32))
           for j in range(2)]
    psO = [p5_stack.enter_context(nc.psum_tensor(f"psO{j}", [128, 512], F32))
           for j in range(2)]
    psW3 = p5_stack.enter_context(nc.psum_tensor("psW3", [128, 2048], F32))

    G.wait(cmt, p4cmt)
    G.wait(cmv, p4cmv)
    Q.wait(cmt, p4cmt)
    Q.wait(cmv, p4cmv)
    Q.wait(dsp, p4dsp)

    T.wait(cmv, p4cmv)
    for n in range(4):
        if n > 0:
            Q.wait(cmt, cmt.n)
        Q.emit(lambda e, n=n: e.dma_start(rowbuf[:, :],
                                          b3_d[None, 512 * n:512 * (n + 1)]),
               inc=dx, inc_n=16)
        T.wait(dx, dx.n)
        T.emit(lambda e, n=n: e.matmul(psW3[:, 512 * n:512 * (n + 1)],
                                       ones1[:, :], rowbuf[:, :],
                                       start=True, stop=True), inc=cmt)
    V.wait(cmt, cmt.n)
    V.emit(lambda e: e.tensor_copy(b3b[:, :], psW3[:, :]), inc=cmv)
    T.wait(cmv, cmv.n)

    Q.wait(dgp, 48)
    for ei in range(8):
        Q.emit(lambda e, ei=ei: e.dma_start(w1t[:, 128 * ei:128 * (ei + 1)],
                                            s_w1b[ei, :, :], transpose=True),
               inc=dtr, inc_n=16)
        Q.emit(lambda e, ei=ei: e.dma_start(w2t[:, 128 * ei:128 * (ei + 1)],
                                            s_w2b[ei, :, :], transpose=True),
               inc=dtr, inc_n=16)
    wexp_ld = dtr.n
    for k in range(NK):
        Q.emit(lambda e, k=k: e.dma_start(w3Tsb[:, k * 2048:(k + 1) * 2048],
                                          s_w3b[:, k * 128:(k + 1) * 128],
                                          transpose=True), inc=dtr, inc_n=16)
    w3t_ld = dtr.n

    p5_free = {}
    gelu_idx = 0
    gelu_cms = {}
    for b in range(B):
        V.wait(cmv, gates_done)
        V.wait(dtr, wexp_ld)
        for ei in range(8):
            V.emit(lambda e, ei=ei, b=b:
                   e.tensor_scalar_mul(w2s[:, ei * 128:(ei + 1) * 128],
                                       w2t[:, ei * 128:(ei + 1) * 128],
                                       gcol[:, b * 8 + ei:b * 8 + ei + 1]))
        V.emit(lambda e, b=b: e.tensor_tensor(tmp8[:, :], b2t[:, :],
                                              gcol[:, b * 8:(b + 1) * 8], ALU.mult),
               inc=cmv)
        V.wait(cmv, cmv.n)
        V.emit(lambda e, b=b: e.reduce_sum(b2mix[:, b:b + 1], tmp8[:, :], AX.X),
               inc=cmv)
        wscale_v = cmv.n
        for i in range(NT):
            t0 = i * TIL
            par = i % 2
            gidx5 = b * NT + i
            xnt5 = xnT5[:, par * 2048:(par + 1) * 2048]
            xt5 = xs5[:, par * D:(par + 1) * D]
            st5 = ss5[:, par * D:(par + 1) * D]
            ot5 = outsb[:, par * D:(par + 1) * D]
            if gidx5 >= 2:
                Q.wait(cmv, p5_free[gidx5 - 2][0])
                Q.wait(dout, p5_free[gidx5 - 2][1])
            for k in range(NK):
                Q.emit(lambda e, d=xnt5, k=k, b=b, t0=t0:
                       e.dma_start(d[:, k * 128:(k + 1) * 128],
                                   s_xnb[b, t0:t0 + TIL, k * 128:(k + 1) * 128],
                                   transpose=True), inc=dtr, inc_n=16)
            Q.emit(lambda e, d=xt5, b=b, t0=t0:
                   e.dma_start(d[:], x_d[b, t0:t0 + TIL, :]), inc=dx, inc_n=16)
            Q.emit(lambda e, d=st5, b=b, t0=t0:
                   e.dma_start(d[:], s_s[b, t0:t0 + TIL, :]), inc=dx, inc_n=16)
            ld5x = dx.n
            ld5t = dtr.n
            T.wait(dtr, ld5t)
            if gidx5 == 0:
                T.wait(dtr, w3t_ld)
            T.wait(cmv, wscale_v)
            for ei in range(8):
                for hg in range(4):
                    j = ei * 4 + hg
                    pj = j % 2
                    if gelu_idx + j >= 2:
                        T.wait(cms, gelu_cms.get(gelu_idx + j - 2, 0))
                    T.emit(lambda e, pj=pj, ei=ei, hg=hg, xnt5=xnt5:
                           e.matmul(psH[pj][:, :], w1t[:, ei * 128:(ei + 1) * 128],
                                    xnt5[:, hg * 512:(hg + 1) * 512],
                                    start=True, stop=True), inc=cmt)
                    mmv = cmt.n
                    SC.wait(cmt, mmv)
                    SC.emit(lambda e, pj=pj, ei=ei, hg=hg:
                            e.activation(Hsb[:, ei * 2048 + hg * 512:
                                             ei * 2048 + (hg + 1) * 512],
                                         psH[pj][:, :], AF.Gelu,
                                         bias=b1t[:, ei:ei + 1]), inc=cms)
                    gelu_cms[gelu_idx + j] = cms.n
            gelu_idx += 32
            T.wait(cms, cms.n)
            for hg in range(4):
                po = hg % 2
                if hg >= 2:
                    T.wait(cmv, cmv.n)
                for ei in range(8):
                    T.emit(lambda e, po=po, ei=ei, hg=hg:
                           e.matmul(psO[po][:, :], w2s[:, ei * 128:(ei + 1) * 128],
                                    Hsb[:, ei * 2048 + hg * 512:
                                        ei * 2048 + (hg + 1) * 512],
                                    start=(ei == 0), stop=(ei == 7)),
                           inc=cmt if ei == 7 else None)
                V.wait(cmt, cmt.n)
                V.emit(lambda e, po=po, hg=hg, xnt5=xnt5, b=b:
                       e.scalar_tensor_tensor(xosb[:, hg * 512:(hg + 1) * 512],
                                              psO[po][:, :], b2mix[:, b:b + 1],
                                              xnt5[:, hg * 512:(hg + 1) * 512],
                                              ALU.add, ALU.add), inc=cmv)
            xo_v = cmv.n
            T.wait(cmv, xo_v)
            for n in range(4):
                for k in range(NK):
                    T.emit(lambda e, n=n, k=k:
                           e.matmul(psW3[:, n * 512:(n + 1) * 512],
                                    xosb[:, k * 128:(k + 1) * 128],
                                    w3Tsb[:, k * 2048 + n * 512:
                                          k * 2048 + (n + 1) * 512],
                                    start=(k == 0), stop=(k == NK - 1)),
                           inc=cmt if (n == 3 and k == NK - 1) else None)
            V.wait(cmt, cmt.n)
            V.emit(lambda e: e.scalar_tensor_tensor(gsb[:, :], psW3[:, :], 1.0,
                                                    b3b[:, :], ALU.mult, ALU.add),
                   inc=cmv)
            gv = cmv.n
            SC.wait(cmv, gv)
            SC.emit(lambda e: e.activation(gsb[:, :], gsb[:, :], AF.Silu), inc=cms)
            V.wait(cms, cms.n)
            V.wait(dx, ld5x)
            if gidx5 >= 2:
                V.wait(dout, p5_free[gidx5 - 2][1])
            V.emit(lambda e, st5=st5: e.tensor_tensor(gsb[:, :], st5[:], gsb[:, :],
                                                      ALU.mult))
            V.emit(lambda e, ot5=ot5, xt5=xt5:
                   e.tensor_tensor(ot5[:], xt5[:], gsb[:, :], ALU.add), inc=cmv)
            ov = cmv.n
            SC.wait(cmv, ov)
            SC.emit(lambda e, ot5=ot5, b=b, t0=t0:
                    e.dma_start(out_d[b, t0:t0 + TIL, :], ot5[:]),
                    inc=dout, inc_n=16)
            p5_free[gidx5] = (ov, dout.n)

    Q.wait(dout, dout.n)

    p5_stack.close()
    return dict(nc=nc, stack=stack, streams=streams)


def finalize(ctx):
    nc = ctx["nc"]
    T, V, SC, Q, G = ctx["streams"]
    check_deadlock(ctx["streams"])
    with nc.Block() as block:
        @block.tensor
        def _(t):
            T.play(t)

        @block.vector
        def _(v):
            V.play(v)

        @block.scalar
        def _(s):
            SC.play(s)

        @block.sync
        def _(q):
            Q.play(q)

        @block.gpsimd
        def _(g):
            G.play(g)
    ctx["stack"].close()
    return nc


KEYMAP = dict(
    x="L0_attn_x", pw="L0_attn_pre_norm_w", wa="L0_attn_w_a", wi="L0_attn_w_i",
    wv="L0_attn_w_v", wop="L0_attn_out_proj_w", db="L0_attn_decay_bias",
    fw="L0_attn_ffn_norm_w", wgu="L0_attn_w_gate_up", wdn="L0_attn_w_down",
    sw="L0_ffn_switch_w", sb="L0_ffn_switch_b", w1="L0_ffn_w1", b1="L0_ffn_b1",
    w2="L0_ffn_w2", b2="L0_ffn_b2", w3="L0_ffn_w3", b3="L0_ffn_b3")


def kernel(**inputs):
    ctx = build(dbg=False)
    nc = finalize(ctx)
    full_x = np.ascontiguousarray(np.asarray(inputs["L0_attn_x"], np.float32))
    weights = {k: np.ascontiguousarray(np.asarray(inputs[v], np.float32))
               for k, v in KEYMAP.items() if k != "x"}
    weights["sw"] = weights["sw"].reshape(8, S * D)
    in_maps = []
    for c in range(8):
        m = dict(weights)
        m["x"] = full_x[2 * c:2 * c + 2]
        in_maps.append(m)
    res = run_bass_kernel_spmd(nc, in_maps, core_ids=list(range(8)))
    return np.concatenate([r["out"] for r in res.results], axis=0)


# version 11
# speedup vs baseline: 1.1395x; 1.1180x over previous
import contextlib
import numpy as np
import concourse.bass as bass
import concourse.mybir as mybir
from concourse.bass_utils import run_bass_kernel_spmd

F32 = mybir.dt.float32
BF16 = mybir.dt.bfloat16
AF = mybir.ActivationFunctionType
ALU = mybir.AluOpType
AX = mybir.AxisListType

CLAMP = 23.02585093
B = 2
S = 2048
D = 2048
TIL = 128
NT = S // TIL
GT = 8
NG = NT // GT
GTOK = GT * TIL
NK = D // 128
FF = 8192
NF = FF // 128
NCH = GTOK // 64


class Sem:
    def __init__(self, stack, nc, name):
        self.h = stack.enter_context(nc.semaphore(name))
        self.n = 0
        self.name = name


class Stream:
    def __init__(self, name):
        self.name = name
        self.ops = []

    def wait(self, sem, val):
        if val > 0:
            self.ops.append(("w", sem, val))

    def emit(self, fn, inc=None, inc_n=1):
        self.ops.append(("o", fn, inc, inc_n))
        if inc is not None:
            inc.n += inc_n

    def play(self, eng):
        for rec in self.ops:
            if rec[0] == "w":
                eng.wait_ge(rec[1].h, rec[2])
            else:
                inst = rec[1](eng)
                if rec[2] is not None:
                    inst.then_inc(rec[2].h, rec[3])


def check_deadlock(streams):
    cnt = {}
    idx = {st.name: 0 for st in streams}
    progress = True
    while progress:
        progress = False
        for st in streams:
            while idx[st.name] < len(st.ops):
                rec = st.ops[idx[st.name]]
                if rec[0] == "w":
                    if cnt.get(rec[1].name, 0) < rec[2]:
                        break
                else:
                    if rec[2] is not None:
                        cnt[rec[2].name] = cnt.get(rec[2].name, 0) + rec[3]
                idx[st.name] += 1
                progress = True
    stuck = {st.name: (idx[st.name], len(st.ops)) for st in streams
             if idx[st.name] < len(st.ops)}
    if stuck:
        details = []
        for st in streams:
            if idx[st.name] < len(st.ops):
                rec = st.ops[idx[st.name]]
                details.append(f"{st.name}@{idx[st.name]}/{len(st.ops)} waiting "
                               f"{rec[1].name}>={rec[2]} (have {cnt.get(rec[1].name, 0)})")
        raise RuntimeError("DEADLOCK: " + "; ".join(details))


def build(dbg=False):
    nc = bass.Bass("TRN2", target_bir_lowering=False)
    stack = contextlib.ExitStack()

    x_d = nc.dram_tensor("x", [B, S, D], F32, kind="ExternalInput")
    pw_d = nc.dram_tensor("pw", [D], F32, kind="ExternalInput")
    wa_d = nc.dram_tensor("wa", [D, D], F32, kind="ExternalInput")
    wi_d = nc.dram_tensor("wi", [D, D], F32, kind="ExternalInput")
    wv_d = nc.dram_tensor("wv", [D, D], F32, kind="ExternalInput")
    wop_d = nc.dram_tensor("wop", [D, D], F32, kind="ExternalInput")
    db_d = nc.dram_tensor("db", [D], F32, kind="ExternalInput")
    fw_d = nc.dram_tensor("fw", [D], F32, kind="ExternalInput")
    wgu_d = nc.dram_tensor("wgu", [D, 2 * FF], F32, kind="ExternalInput")
    wdn_d = nc.dram_tensor("wdn", [FF, D], F32, kind="ExternalInput")
    sw_d = nc.dram_tensor("sw", [8, S * D], F32, kind="ExternalInput")
    sb_d = nc.dram_tensor("sb", [8], F32, kind="ExternalInput")
    w1_d = nc.dram_tensor("w1", [8, 128, 128], F32, kind="ExternalInput")
    b1_d = nc.dram_tensor("b1", [8, 128], F32, kind="ExternalInput")
    w2_d = nc.dram_tensor("w2", [8, 128, 128], F32, kind="ExternalInput")
    b2_d = nc.dram_tensor("b2", [8, 128], F32, kind="ExternalInput")
    w3_d = nc.dram_tensor("w3", [D, D], F32, kind="ExternalInput")
    b3_d = nc.dram_tensor("b3", [D], F32, kind="ExternalInput")
    out_d = nc.dram_tensor("out", [B, S, D], F32, kind="ExternalOutput")

    ik = "ExternalOutput" if dbg else "Internal"
    s_xn = nc.dram_tensor("s_xn", [B, S, D], F32, kind=ik)
    s_xnb = nc.dram_tensor("s_xnb", [B, S, D], BF16, kind="Internal")
    s_h = nc.dram_tensor("s_h", [B, S, D], BF16, kind=ik)
    s_y = nc.dram_tensor("s_y", [B, NG, NK, 128, GTOK], BF16, kind=ik)
    s_x2 = nc.dram_tensor("s_x2", [B, S, D], F32, kind=ik)
    s_h2 = nc.dram_tensor("s_h2", [B, S, D], BF16, kind="Internal")
    s_z = nc.dram_tensor("s_z", [B, NG, NF, 128, GTOK], BF16, kind=ik)
    s_xo = nc.dram_tensor("s_xo", [B, NT, 128, 2048], BF16, kind=ik)
    s_wt_aiv = [nc.dram_tensor(f"s_wt_aiv{j}", [128, NK * 2048], BF16,
                               kind="Internal") for j in range(3)]
    s_wt_g = nc.dram_tensor("s_wt_g", [128, NF * 2048], BF16, kind="Internal")
    s_wt_u = nc.dram_tensor("s_wt_u", [128, NF * 2048], BF16, kind="Internal")
    s_wt_dn = nc.dram_tensor("s_wt_dn", [4, 128, NF * 512], BF16, kind="Internal")
    s_w1b = nc.dram_tensor("s_w1b", [8, 128, 128], BF16, kind="Internal")
    s_w2b = nc.dram_tensor("s_w2b", [8, 128, 128], BF16, kind="Internal")
    s_w3b = nc.dram_tensor("s_w3b", [D, D], BF16, kind="Internal")
    d_gate = nc.dram_tensor("d_gate", [128, 8 * B], F32,
                            kind=ik) if dbg else None

    T = Stream("T")
    V = Stream("V")
    SC = Stream("SC")
    Q = Stream("Q")
    G = Stream("G")
    streams = [T, V, SC, Q, G]

    def mksem(name):
        return Sem(stack, nc, name)

    cmt = mksem("cmt")
    cmv = mksem("cmv")
    cms = mksem("cms")
    dx = mksem("dx")
    dsp = mksem("dsp")
    dgp = mksem("dgp")
    dtr = mksem("dtr")
    dwl = mksem("dwl")
    dtl3 = mksem("dtl3")
    dtl4 = mksem("dtl4")
    dwop = mksem("dwop")
    drt = mksem("drt")
    cmg = mksem("cmg")
    dpw = mksem("dpw")
    dps = [mksem(f"dps{i}") for i in range(2)]
    dpsA = [mksem(f"dpsA{i}") for i in range(2)]
    dpsB = [mksem(f"dpsB{i}") for i in range(2)]
    dout = mksem("dout")
    cpe = [mksem(f"cpe{i}") for i in range(4)]

    sbt = lambda name, shape, dt: stack.enter_context(nc.sbuf_tensor(name, shape, dt))
    ones1 = sbt("ones1", [1, 128], F32)
    ones64 = sbt("ones64", [128, 64], F32)
    ndb = sbt("ndb", [128, NK], F32)
    stc = sbt("stc", [128, NK], F32)
    pwb = sbt("pwb", [128, D], BF16)
    rowbuf = sbt("rowbuf", [1, 512], F32)
    racc = sbt("racc", [128, 8 * B], F32)
    junk = sbt("junk", [128, D], F32)
    colp = sbt("colp", [128, 16], F32)
    lgrow = sbt("lgrow", [1, 16], F32)
    sbrow = sbt("sbrow", [1, 8], F32)
    lgb = sbt("lgb", [128, 8 * B], F32)
    gcol = sbt("gcol", [128, 8 * B], F32)
    b1t = sbt("b1t", [128, 8], F32)
    b2t = sbt("b2t", [128, 8], F32)
    b2mix = sbt("b2mix", [128, B], F32)
    tmp8 = sbt("tmp8", [128, 8], F32)
    stg = sbt("stg", [128, 2 * 1024], F32)
    d0p = sbt("d0p", [128, 512], F32)
    gdcol = sbt("gdcol", [128, 1], F32)

    V.emit(lambda e: e.memset(ones1[:, :], 1.0))
    V.emit(lambda e: e.memset(ones64[:, :], 1.0))
    V.emit(lambda e: e.memset(racc[:, :], 0.0))
    V.emit(lambda e: e.memset(stc[:, :], 0.0))
    V.emit(lambda e: e.memset(d0p[:, :], 1.0), inc=cmv)
    V.wait(cmv, cmv.n)
    V.emit(lambda e: e.memset(
        bass.AP(d0p.ap().tensor, d0p.ap().offset + 64,
                [d0p.ap().ap[0], [64, 7]]), 0.0), inc=cmv)

    Q.emit(lambda e: e.dma_start(ndb[:, :], db_d.rearrange("(k p) -> p k", p=128),
                                 allow_slow_non_contiguous=True),
           inc=dx, inc_n=16)
    for ei in range(8):
        Q.emit(lambda e, ei=ei: e.dma_start(b1t[:, ei:ei + 1], b1_d[ei, :, None],
                                            allow_slow_non_contiguous=True),
               inc=dx, inc_n=16)
        Q.emit(lambda e, ei=ei: e.dma_start(b2t[:, ei:ei + 1], b2_d[ei, :, None],
                                            allow_slow_non_contiguous=True),
               inc=dx, inc_n=16)
    Q.emit(lambda e: e.dma_start(sbrow[:, :], sb_d[None, :]), inc=dx, inc_n=16)
    V.wait(dx, 16)
    V.emit(lambda e: e.tensor_scalar_mul(ndb[:, :], ndb[:, :], -1.0), inc=cmv)

    with nc.psum_tensor("psb", [128, 512], F32) as psb:
        T.wait(cmv, cmv.n)
        for n in range(4):
            if n > 0:
                Q.wait(cmt, cmt.n)
            Q.emit(lambda e, n=n: e.dma_start(rowbuf[:, :],
                                              pw_d[None, 512 * n:512 * (n + 1)]),
                   inc=dx, inc_n=16)
            T.wait(dx, dx.n)
            T.wait(cmv, cmv.n)
            T.emit(lambda e, n=n: e.matmul(psb[:, :], ones1[:, :], rowbuf[:, :],
                                           start=True, stop=True), inc=cmt)
            V.wait(cmt, cmt.n)
            V.emit(lambda e, n=n: e.tensor_copy(pwb[:, 512 * n:512 * (n + 1)],
                                                psb[:, :]), inc=cmv)
    pwb_done = cmv.n

    G.emit(lambda e: e.dma_start(s_w1b[:, :, :], w1_d[:, :, :]), inc=dgp, inc_n=16)
    G.emit(lambda e: e.dma_start(s_w2b[:, :, :], w2_d[:, :, :]), inc=dgp, inc_n=16)
    G.emit(lambda e: e.dma_start(s_w3b[:, :], w3_d[:, :]), inc=dgp, inc_n=16)

    groups = [(b, gi) for b in range(B) for gi in range(NG)]
    xn_done = {}
    dot_done = {}
    h_spill = {}
    hT_done = {}
    chain_mm = {}
    npool = [0, 0, 0, 0]
    wl_target = {}
    xnbb_sp = {}
    y_prev_sp = {}
    y_sp_last = {}
    unit_last_v = {}
    unit_last_s = {}
    wmats = [wa_d, wi_d, wv_d]

    p1 = {}
    p1_stack = contextlib.ExitStack()

    def p1_alloc():
        es = p1_stack.enter_context
        p1["hTg"] = es(nc.sbuf_tensor("hTg", [128, 2 * NK * GTOK], BF16))
        p1["ybuf"] = es(nc.sbuf_tensor("ybuf", [128, 2 * 512], BF16))
        p1["d0u"] = es(nc.sbuf_tensor("d0u", [128, 4 * 512], F32))
        p1["xsb"] = es(nc.sbuf_tensor("xsb", [128, 2 * D], F32))
        p1["hsb"] = es(nc.sbuf_tensor("hsb", [128, 2 * D], BF16))
        p1["xnbb"] = es(nc.sbuf_tensor("xnbb", [128, D], BF16))
        p1["wsb"] = es(nc.sbuf_tensor("wsb", [128, 6 * NK * 128], BF16))
        p1["sc"] = es(nc.sbuf_tensor("scp", [128, 12 * 1024], F32))
        p1["swr"] = es(nc.sbuf_tensor("swr", [128, 2 * 1024], F32))
        for j in range(4):
            p1[f"pp{j}"] = es(nc.psum_tensor(f"pp{j}", [128, 1024], F32))

    p1_alloc()

    def stage_a(b, gi, i):
        t0 = gi * GTOK + i * TIL
        par = i % 2
        xt = p1["xsb"][:, par * D:(par + 1) * D]
        ht = p1["hsb"][:, par * D:(par + 1) * D]
        gidx = groups.index((b, gi))
        prev = None
        if i >= 2:
            prev = (b, gi, i - 2)
        elif gidx > 0:
            pb, pg = groups[gidx - 1]
            prev = (pb, pg, GT - 2 + i)
        if prev is not None:
            Q.wait(dsp, xn_done[prev][1])
            if prev in dot_done:
                Q.wait(cmv, dot_done[prev])
        Q.emit(lambda e: e.dma_start(xt[:], x_d[b, t0:t0 + TIL, :]), inc=dx, inc_n=16)
        xld = dx.n
        SC.wait(dx, xld)
        c0 = colp[:, par * 5 + 0:par * 5 + 1]
        c2 = colp[:, par * 5 + 2:par * 5 + 3]
        c3 = colp[:, par * 5 + 3:par * 5 + 4]
        c4 = colp[:, par * 5 + 4:par * 5 + 5]
        SC.emit(lambda e: e.activation(junk[:, :], xt[:], AF.Copy, accum_out=c0),
                inc=cms)
        V.wait(cms, cms.n)
        V.emit(lambda e: e.tensor_scalar_mul(c0, c0, -1.0 / D), inc=cmv)
        V.wait(cmv, cmv.n)
        V.emit(lambda e: e.tensor_scalar_add(xt[:], xt[:], c0), inc=cmv)
        SC.wait(cmv, cmv.n)
        SC.emit(lambda e: e.activation(junk[:, :], xt[:], AF.Square, accum_out=c2),
                inc=cms)
        V.wait(cms, cms.n)
        V.emit(lambda e: e.tensor_scalar(c3, c2, 1.0 / D, 1e-5, ALU.mult, ALU.add),
               inc=cmv)
        SC.wait(cmv, cmv.n)
        SC.emit(lambda e: e.activation(c3, c3, AF.Ln), inc=cms)
        SC.wait(cms, cms.n)
        SC.emit(lambda e: e.activation(c3, c3, AF.Exp, scale=-0.5), inc=cms)
        V.wait(cms, cms.n)
        V.emit(lambda e: e.tensor_scalar(c4, c2, 1.0 / D, c3, ALU.mult, ALU.mult),
               inc=cmv)
        V.wait(cmv, cmv.n)
        V.emit(lambda e: e.tensor_scalar(c4, c4, c3, 1e-6, ALU.mult, ALU.add),
               inc=cmv)
        SC.wait(cmv, cmv.n)
        SC.emit(lambda e: e.activation(c4, c4, AF.Ln), inc=cms)
        SC.wait(cms, cms.n)
        SC.emit(lambda e: e.activation(c4, c4, AF.Exp, scale=-0.5), inc=cms)
        V.wait(cms, cms.n)
        V.emit(lambda e: e.tensor_scalar_mul(xt[:], xt[:], c3), inc=cmv)
        xnv = cmv.n
        if prev is not None:
            V.wait(dsp, xn_done[prev][1])
        V.emit(lambda e: e.scalar_tensor_tensor(ht[:], xt[:], c4, pwb[:, :],
                                                ALU.mult, ALU.mult), inc=cmv)
        hv = cmv.n
        SC.wait(cmv, hv)
        SC.emit(lambda e: e.dma_start(s_xn[b, t0:t0 + TIL, :], xt[:]),
                inc=dsp, inc_n=16)
        SC.emit(lambda e: e.dma_start(s_h[b, t0:t0 + TIL, :], ht[:]),
                inc=dsp, inc_n=16)
        gti = groups.index((b, gi)) * GT + i
        if gti > 0:
            V.wait(dsp, xnbb_sp[gti - 1])
        V.emit(lambda e: e.tensor_copy(p1["xnbb"][:, :], xt[:]), inc=cmv)
        SC.wait(cmv, cmv.n)
        SC.emit(lambda e, b=b, t0=t0: e.dma_start(s_xnb[b, t0:t0 + TIL, :],
                                                  p1["xnbb"][:, :]),
                inc=dsp, inc_n=16)
        xnbb_sp[gti] = dsp.n
        xn_sp = dsp.n
        xn_done[(b, gi, i)] = (xnv, xn_sp)

    rt_unit = [0]
    rt_cmg = {}

    def emit_dots(b, gi, i):
        t = gi * GT + i
        xt = p1["xsb"][:, (i % 2) * D:(i % 2 + 1) * D]
        xnv = xn_done[(b, gi, i)][0]
        for e in range(8):
            for h in range(2):
                j = rt_unit[0]
                rt_unit[0] += 1
                slot = j % 2
                if j >= 2:
                    Q.wait(cmv, rt_cmg[j - 2])
                src = sw_d[e, t * 128 * D:(t + 1) * 128 * D].rearrange(
                    "(p d) -> p d", d=D)[:, h * 1024:(h + 1) * 1024]
                Q.emit(lambda e_, d=p1["swr"][:, slot * 1024:(slot + 1) * 1024],
                       s=src: e_.dma_start(d, s), inc=drt, inc_n=16)
                V.wait(drt, drt.n)
                V.wait(cmv, xnv)
                V.emit(lambda e_, xh=xt[:, h * 1024:(h + 1) * 1024],
                       sl=p1["swr"][:, slot * 1024:(slot + 1) * 1024]:
                       e_.scalar_tensor_tensor(junk[:, 0:1024], xh, 1.0, sl,
                                               ALU.mult, ALU.mult,
                                               accum_out=gdcol[:, :]), inc=cmv)
                V.wait(cmv, cmv.n)
                V.emit(lambda e_, b=b, e=e: e_.tensor_tensor(
                    racc[:, b * 8 + e:b * 8 + e + 1],
                    racc[:, b * 8 + e:b * 8 + e + 1], gdcol[:, :], ALU.add),
                    inc=cmv)
                rt_cmg[j] = cmv.n
        dot_done[(b, gi, i)] = cmv.n

    def transposes(b, gi):
        gidx = groups.index((b, gi))
        gbuf = gidx % 2
        Q.wait(dsp, h_spill[(b, gi)])
        if gidx >= 2:
            pb, pg = groups[gidx - 2]
            Q.wait(cmt, chain_mm[(pb, pg, 3 * NK - 1)])
        for k in range(NK):
            dst = p1["hTg"][:, gbuf * NK * GTOK + k * GTOK:
                            gbuf * NK * GTOK + (k + 1) * GTOK]
            src = s_h[b, gi * GTOK:(gi + 1) * GTOK, k * 128:(k + 1) * 128]
            Q.emit(lambda e, d=dst, s=src: e.dma_start(d, s, transpose=True),
                   inc=dtr, inc_n=16)
        hT_done[(b, gi)] = dtr.n

    def chain_cmt(gci):
        return cmt_base + gci + 1

    p0w_units = []
    for m in range(NK):
        for wj in range(3):
            for h in range(2):
                srcap = wmats[wj][h * 1024:(h + 1) * 1024,
                                  m * 128:(m + 1) * 128].rearrange(
                    "(k p) c -> p k c", p=128)
                dst = s_wt_aiv[wj][:, m * 2048 + h * 1024:
                                   m * 2048 + h * 1024 + 1024].rearrange(
                    "p (k c) -> p k c", c=128)
                p0w_units.append((srcap, dst))
    N_AIV_U = len(p0w_units)
    for mp in range(NF):
        for half_t, base in ((s_wt_g, 0), (s_wt_u, FF)):
            for h in range(2):
                srcap = wgu_d[h * 1024:(h + 1) * 1024,
                              base + mp * 128:base + (mp + 1) * 128].rearrange(
                    "(k p) c -> p k c", p=128)
                dst = half_t[:, mp * 2048 + h * 1024:
                             mp * 2048 + h * 1024 + 1024].rearrange(
                    "p (k c) -> p k c", c=128)
                p0w_units.append((srcap, dst))
    N_GU_U = len(p0w_units) - N_AIV_U
    for q in range(4):
        for k2 in range(32):
            srcap = wdn_d[k2 * 256:(k2 + 1) * 256,
                          q * 512:(q + 1) * 512].rearrange(
                "(k p) n -> p k n", p=128)
            dst = s_wt_dn[q][:, k2 * 1024:(k2 + 1) * 1024].rearrange(
                "p (k n) -> p k n", n=512)
            p0w_units.append((srcap, dst))
    p0w_i = [0]

    def drain_p0w(n):
        for _ in range(n):
            j = p0w_i[0]
            if j >= len(p0w_units):
                return
            p0w_i[0] += 1
            srcap, dst = p0w_units[j]
            if j < 24:
                ssem = dpsA[j % 2]
            elif j < N_AIV_U:
                ssem = dpsB[j % 2]
            else:
                ssem = dps[j % 2]
            G.emit(lambda e, d=dst, s=srcap: e.dma_start(d, s), inc=ssem, inc_n=16)

    p1_tl = []
    for gidx0, (b0_, gi0_) in enumerate(groups):
        for ci in range(3 * NK):
            m, wj = divmod(ci, 3)
            gci = gidx0 * 3 * NK + ci
            buf = gci % 6
            dst = p1["wsb"][:, buf * NK * 128:(buf + 1) * NK * 128]
            srcap = s_wt_aiv[wj][:, m * 2048:(m + 1) * 2048]
            pu = 2 * (m * 3 + wj) + 2
            p1_tl.append((dst, srcap, gci, pu))
    p1_tl_i = [0]

    def drain_tl(n):
        for _ in range(n):
            j = p1_tl_i[0]
            if j >= len(p1_tl):
                return
            p1_tl_i[0] += 1
            dst, srcap, gci, pu = p1_tl[j]
            assert gci == j
            if gci < 12:
                Q.wait(dpsA[0], 16 * 12)
                Q.wait(dpsA[1], 16 * 12)
            elif gci < 3 * NK:
                Q.wait(dpsA[0], 16 * 12)
                Q.wait(dpsA[1], 16 * 12)
                Q.wait(dpsB[0], 16 * ((N_AIV_U - 24) // 2))
                Q.wait(dpsB[1], 16 * ((N_AIV_U - 24) // 2))
            if gci >= 6:
                Q.wait(cmt, chain_cmt(gci - 6))
            Q.emit(lambda e, d=dst, s=srcap: e.dma_start(d, s), inc=dwl, inc_n=16)

    def mm_chains(b, gi):
        gidx = groups.index((b, gi))
        gbuf = gidx % 2
        T.wait(dtr, hT_done[(b, gi)])
        for ci in range(3 * NK):
            m, wj = divmod(ci, 3)
            gci = gidx * 3 * NK + ci
            buf = gci % 6
            pool = ci % 4
            T.wait(dwl, 16 * (gci + 1))
            if npool[pool] > 0:
                T.wait(cpe[pool], npool[pool])
            ps = p1[f"pp{pool}"]
            wt = p1["wsb"][:, buf * NK * 128:(buf + 1) * NK * 128]
            for k in range(NK):
                for hh in range(2):
                    last = (k == NK - 1) and (hh == 1)
                    rhs = p1["hTg"][:, gbuf * NK * GTOK + k * GTOK + hh * 512:
                                    gbuf * NK * GTOK + k * GTOK + hh * 512 + 512]
                    T.emit(lambda e, ps=ps, w=wt[:, k * 128:(k + 1) * 128], r=rhs,
                           hh=hh, k=k:
                           e.matmul(ps[:, hh * 512:hh * 512 + 512], w, r,
                                    start=(k == 0), stop=(k == NK - 1)),
                           inc=cmt if last else None)
            chain_mm[(b, gi, ci)] = cmt.n
            assert cmt.n == chain_cmt(gci)
            npool[pool] += 1

    SUW = 512
    SCH = SUW // 64

    def subpool(su):
        p = su % 4
        return [p1["sc"][:, (p * 6 + j) * SUW:(p * 6 + j + 1) * SUW]
                for j in range(6)], p1["d0u"][:, (su % 4) * SUW:(su % 4 + 1) * SUW]

    def su_of(m, h):
        return 2 * m + h

    def sphase_a(b, gi, m, h):
        su = su_of(m, h)
        bs, _ = subpool(su)
        prev = None
        if su >= 4:
            prev = (b, gi, su - 4)
        else:
            gidx = groups.index((b, gi))
            if gidx > 0:
                pb, pg = groups[gidx - 1]
                prev = (pb, pg, 2 * NK - 4 + su)
        if prev is not None:
            SC.wait(cmv, unit_last_v[prev])
        SC.wait(cmt, chain_mm[(b, gi, 3 * m + 1)])
        pa = p1[f"pp{(3 * m) % 4}"][:, h * SUW:(h + 1) * SUW]
        pi_ = p1[f"pp{(3 * m + 1) % 4}"][:, h * SUW:(h + 1) * SUW]
        ndbc = ndb[:, m:m + 1]
        SC.emit(lambda e: e.activation(bs[0][:, :], pa, AF.Exp, bias=ndbc,
                                       scale=-1.0))
        SC.emit(lambda e: e.activation(bs[1][:, :], pi_, AF.Exp, scale=-1.0),
                inc=cms)
        if h == 1:
            SC.emit(lambda e: e.activation(colp[:, 10:11], colp[:, 10:11], AF.Copy),
                    inc=cpe[(3 * m) % 4])
            SC.emit(lambda e: e.activation(colp[:, 10:11], colp[:, 10:11], AF.Copy),
                    inc=cpe[(3 * m + 1) % 4])
        return cms.n

    def vphase_a(b, gi, m, h, sa):
        su = su_of(m, h)
        bs, _ = subpool(su)
        V.wait(cms, sa)
        V.wait(cmt, chain_mm[(b, gi, 3 * m + 2)])
        pv = p1[f"pp{(3 * m + 2) % 4}"][:, h * SUW:(h + 1) * SUW]
        V.emit(lambda e: e.tensor_copy(bs[2][:, :], pv),
               inc=cpe[(3 * m + 2) % 4] if h == 1 else None)
        V.emit(lambda e: e.tensor_scalar_add(bs[3][:, :], bs[0][:, :], 1.0))
        V.emit(lambda e: e.tensor_scalar_add(bs[4][:, :], bs[1][:, :], 1.0), inc=cmv)
        return cmv.n

    def sphase_b(b, gi, m, h, va):
        su = su_of(m, h)
        bs, _ = subpool(su)
        SC.wait(cmv, va)
        SC.emit(lambda e: e.activation(bs[0][:, :], bs[3][:, :], AF.Ln))
        SC.emit(lambda e: e.activation(bs[1][:, :], bs[4][:, :], AF.Ln), inc=cms)
        return cms.n

    def vphase_b(b, gi, m, h, sb_):
        su = su_of(m, h)
        bs, _ = subpool(su)
        V.wait(cms, sb_)
        V.emit(lambda e: e.tensor_scalar_min(bs[3][:, :], bs[0][:, :], CLAMP),
               inc=cmv)
        V.wait(cmv, cmv.n)
        V.emit(lambda e: e.tensor_tensor_scan(bs[4][:, :], d0p[:, :], bs[3][:, :],
                                              0.0, ALU.mult, ALU.add))
        V.emit(lambda e: e.tensor_scalar_min(bs[5][:, :], bs[4][:, :], CLAMP),
               inc=cmv)
        return cmv.n

    def sphase_c(b, gi, m, h, vb):
        su = su_of(m, h)
        bs, _ = subpool(su)
        SC.wait(cmv, vb)
        SC.emit(lambda e: e.activation(bs[0][:, :], bs[3][:, :], AF.Exp, scale=-2.0))
        SC.emit(lambda e: e.activation(bs[3][:, :], bs[4][:, :], AF.Exp, scale=-1.0),
                inc=cms)
        return cms.n

    def vphase_c(b, gi, m, h, sc_):
        su = su_of(m, h)
        bs, _ = subpool(su)
        V.wait(cms, sc_)
        V.emit(lambda e: e.tensor_scalar(bs[0][:, :], bs[0][:, :], -1.0, 1.0,
                                         ALU.mult, ALU.add))
        V.emit(lambda e: e.tensor_scalar_max(bs[0][:, :], bs[0][:, :], 1e-8),
               inc=cmv)
        return cmv.n

    def sphase_d(b, gi, m, h, vc):
        su = su_of(m, h)
        bs, _ = subpool(su)
        SC.wait(cmv, vc)
        SC.emit(lambda e: e.activation(bs[0][:, :], bs[0][:, :], AF.Ln), inc=cms)
        return cms.n

    def vphase_d(b, gi, m, h, sd):
        su = su_of(m, h)
        bs, _ = subpool(su)
        V.wait(cms, sd)
        V.emit(lambda e: e.scalar_tensor_tensor(bs[1][:, :], bs[0][:, :], 0.5,
                                                bs[1][:, :], ALU.mult, ALU.subtract),
               inc=cmv)
        V.wait(cmv, cmv.n)
        V.emit(lambda e: e.tensor_tensor(bs[1][:, :], bs[1][:, :], bs[5][:, :],
                                         ALU.add), inc=cmv)
        return cmv.n

    def sphase_e(b, gi, m, h, vd):
        su = su_of(m, h)
        bs, _ = subpool(su)
        SC.wait(cmv, vd)
        SC.emit(lambda e: e.activation(bs[1][:, :], bs[1][:, :], AF.Exp), inc=cms)
        unit_last_s[(b, gi, su)] = cms.n
        return cms.n

    def vphase_e(b, gi, m, h, se):
        su = su_of(m, h)
        bs, d0u = subpool(su)
        V.wait(cms, se)
        V.emit(lambda e: e.tensor_tensor(bs[2][:, :], bs[1][:, :], bs[2][:, :],
                                         ALU.mult))
        stcol = stc[:, m:m + 1]
        V.emit(lambda e, d0u=d0u: e.tensor_copy(
            bass.AP(d0u.tensor, d0u.offset + 64, [d0u.ap[0], [64, SCH - 1]]),
            bass.AP(bs[3].tensor, bs[3].offset + 63, [bs[3].ap[0], [64, SCH - 1]])),
            inc=cmv)
        V.wait(cmv, cmv.n)
        V.emit(lambda e, d0u=d0u: e.tensor_tensor_scan(bs[4][:, :], d0u[:, :],
                                                       bs[2][:, :], stcol,
                                                       ALU.mult, ALU.add))
        ydst = p1["ybuf"][:, (su % 2) * SUW:(su % 2 + 1) * SUW]
        if (b, gi, su) in y_prev_sp:
            V.wait(dsp, y_prev_sp[(b, gi, su)])
        V.emit(lambda e, ydst=ydst: e.tensor_tensor(ydst, bs[3][:, :], bs[4][:, :],
                                                    ALU.mult))
        V.emit(lambda e: e.tensor_tensor(stcol, bs[3][:, SUW - 1:SUW],
                                         bs[4][:, SUW - 1:SUW], ALU.mult), inc=cmv)
        unit_last_v[(b, gi, su)] = cmv.n
        SC.wait(cmv, cmv.n)
        SC.emit(lambda e, b=b, gi=gi, m=m, h=h, ydst=ydst: e.dma_start(
            s_y[b, gi, m, :, h * SUW:(h + 1) * SUW], ydst), inc=dsp, inc_n=16)
        nsu = su + 2
        nkey = None
        if nsu < 2 * NK:
            nkey = (b, gi, nsu)
        else:
            gg = groups.index((b, gi))
            if gg + 1 < len(groups):
                nb_, ngi_ = groups[gg + 1]
                nkey = (nb_, ngi_, nsu - 2 * NK)
        if nkey is not None:
            y_prev_sp[nkey] = dsp.n
        y_sp_last[(b, gi)] = dsp.n
        return cmv.n

    def scan_quad(b, gi, m0):
        sus = [(m0, 0), (m0, 1), (m0 + 1, 0), (m0 + 1, 1)]
        sa = [sphase_a(b, gi, m, h) for m, h in sus]
        va = [vphase_a(b, gi, m, h, sa[j]) for j, (m, h) in enumerate(sus)]
        sb_ = [sphase_b(b, gi, m, h, va[j]) for j, (m, h) in enumerate(sus)]
        vb = [vphase_b(b, gi, m, h, sb_[j]) for j, (m, h) in enumerate(sus)]
        sc_ = [sphase_c(b, gi, m, h, vb[j]) for j, (m, h) in enumerate(sus)]
        vc = [vphase_c(b, gi, m, h, sc_[j]) for j, (m, h) in enumerate(sus)]
        sd = [sphase_d(b, gi, m, h, vc[j]) for j, (m, h) in enumerate(sus)]
        vd = [vphase_d(b, gi, m, h, sd[j]) for j, (m, h) in enumerate(sus)]
        se = [sphase_e(b, gi, m, h, vd[j]) for j, (m, h) in enumerate(sus)]
        for j, (m, h) in enumerate(sus):
            vphase_e(b, gi, m, h, se[j])

    V.emit(lambda e: e.memset(p1["d0u"][:, :], 1.0), inc=cmv)
    cmt_base = cmt.n
    y_spill = {}
    for gidx, (b, gi) in enumerate(groups):
        if gidx == 0:
            V.wait(cmv, pwb_done)
            T.wait(cmv, pwb_done)
            for i in range(GT):
                stage_a(b, gi, i)
                emit_dots(b, gi, i)
            h_spill[(b, gi)] = dsp.n
            transposes(b, gi)
            drain_p0w(N_AIV_U + N_GU_U)
            drain_tl(10)
        if gi == 0 and b > 0:
            V.emit(lambda e: e.memset(stc[:, :], 0.0), inc=cmv)
        mm_chains(b, gi)
        if gidx + 1 < len(groups):
            nb, ngi = groups[gidx + 1]
        else:
            nb = None
        for mp in range(NK // 2):
            scan_quad(b, gi, 2 * mp)
            if nb is not None and mp < GT // 2:
                stage_a(nb, ngi, 2 * mp)
                emit_dots(nb, ngi, 2 * mp)
                stage_a(nb, ngi, 2 * mp + 1)
                emit_dots(nb, ngi, 2 * mp + 1)
                if mp == GT // 2 - 1:
                    h_spill[(nb, ngi)] = dsp.n
                    transposes(nb, ngi)
            drain_tl(6)
        y_spill[(b, gi)] = y_sp_last[(b, gi)]
    assert p1_tl_i[0] == len(p1_tl), (p1_tl_i, len(p1_tl))

    p1_stack.close()
    p1cmt, p1cmv, p1cms, p1dsp = cmt.n, cmv.n, cms.n, dsp.n

    p2_stack = contextlib.ExitStack()
    sb2 = lambda name, shape, dt: p2_stack.enter_context(
        nc.sbuf_tensor(name, shape, dt))
    wopsb = sb2("wopsb", [128, NK * 2048], BF16)
    fwb = sb2("fwb", [128, D], F32)
    ysb = sb2("ysb", [128, 2 * 2048], BF16)
    xnb2 = sb2("xnb2", [128, 2 * D], F32)
    h2b = sb2("h2b", [128, 2 * D], BF16)
    xnT5 = sb2("xnT5", [128, 2 * 2048], BF16)
    Hsb = sb2("Hsb", [128, 8 * 512], BF16)
    xosb = sb2("xosb", [128, 2048], BF16)
    w1t = sb2("w1t", [128, 8 * 128], BF16)
    w2t = sb2("w2t", [128, 8 * 128], BF16)
    w1sel = sb2("w1sel", [128, 4 * 128], BF16)
    w2gsel = sb2("w2gsel", [128, 4 * 128], BF16)
    b1sel = sb2("b1sel", [128, 4], F32)
    msel = sb2("msel", [128, 32], F32)
    psW2 = p2_stack.enter_context(nc.psum_tensor("psW2", [128, 2048], F32))
    psE = [p2_stack.enter_context(nc.psum_tensor(f"psE{j}", [128, 512], F32))
           for j in range(4)]

    G.wait(cmt, p1cmt)
    G.wait(cmv, p1cmv)
    Q.wait(cmt, p1cmt)
    Q.wait(cmv, p1cmv)
    Q.wait(dsp, p1dsp)

    for kb in range(4):
        src = wop_d[kb * 512:(kb + 1) * 512, :].rearrange("(k p) n -> p k n", p=128)
        dst = wopsb[:, kb * 4 * 2048:(kb + 1) * 4 * 2048].rearrange(
            "p (k n) -> p k n", k=4)
        G.emit(lambda e, d=dst, s=src: e.dma_start(d, s), inc=dwop, inc_n=16)
    wop_ld = dwop.n

    Q.wait(dgp, 48)
    for ei in range(8):
        Q.emit(lambda e, ei=ei: e.dma_start(w1t[:, 128 * ei:128 * (ei + 1)],
                                            s_w1b[ei, :, :], transpose=True),
               inc=dtr, inc_n=16)
        Q.emit(lambda e, ei=ei: e.dma_start(w2t[:, 128 * ei:128 * (ei + 1)],
                                            s_w2b[ei, :, :], transpose=True),
               inc=dtr, inc_n=16)
    wexp_ld = dtr.n

    pE0 = psE[0]
    T.wait(cms, p1cms)
    T.wait(cmv, p1cmv)
    V.wait(cms, p1cms)
    for b in range(B):
        T.emit(lambda e, b=b: e.matmul(pE0[0:1, b * 8:b * 8 + 8],
                                       ones64[:, 0:1], racc[:, b * 8:b * 8 + 8],
                                       start=True, stop=True), inc=cmt)
        V.wait(cmt, cmt.n)
        V.emit(lambda e, b=b: e.tensor_copy(lgrow[0:1, b * 8:b * 8 + 8],
                                            pE0[0:1, b * 8:b * 8 + 8]), inc=cmv)
        V.wait(cmv, cmv.n)
        V.emit(lambda e, b=b: e.tensor_tensor(lgrow[0:1, b * 8:b * 8 + 8],
                                              lgrow[0:1, b * 8:b * 8 + 8],
                                              sbrow[0:1, :], ALU.add), inc=cmv)
        T.wait(cmv, cmv.n)
        T.emit(lambda e, b=b: e.matmul(pE0[:, b * 8:b * 8 + 8], ones1[:, :],
                                       lgrow[0:1, b * 8:b * 8 + 8],
                                       start=True, stop=True), inc=cmt)
        V.wait(cmt, cmt.n)
        lgs = lgb[:, b * 8:(b + 1) * 8]
        V.emit(lambda e, b=b, lgs=lgs: e.tensor_copy(lgs, pE0[:, b * 8:b * 8 + 8]),
               inc=cmv)
        V.wait(cmv, cmv.n)
        c12 = colp[:, 12:13]
        c13 = colp[:, 13:14]
        c14 = colp[:, 14:15]
        def vsem(fn):
            V.emit(fn, inc=cmv)
            V.wait(cmv, cmv.n)
        vsem(lambda e, lgs=lgs, c=c12: e.reduce_max(c, lgs, AX.X))
        vsem(lambda e, c=c12: e.tensor_scalar_mul(c, c, -1.0))
        SC.wait(cmv, cmv.n)
        SC.emit(lambda e, lgs=lgs, c=c12: e.activation(lgs, lgs, AF.Exp, bias=c),
                inc=cms)
        V.wait(cms, cms.n)
        vsem(lambda e, lgs=lgs, c=c13: e.reduce_sum(c, lgs, AX.X))
        vsem(lambda e, c=c13: e.reciprocal(c, c))
        vsem(lambda e, lgs=lgs, c=c13: e.tensor_scalar_mul(lgs, lgs, c))
        for ei in range(8):
            vsem(lambda e, lgs=lgs, ei=ei:
                 e.tensor_single_scalar(tmp8[:, :], lgs, lgs[:, ei:ei + 1],
                                        ALU.is_gt))
            vsem(lambda e, c=c14: e.reduce_sum(c, tmp8[:, :], AX.X))
            vsem(lambda e, c=c14, o=c12: e.tensor_single_scalar(o, c, 1.5,
                                                                ALU.is_lt))
            vsem(lambda e, c=c14, o=c13: e.tensor_single_scalar(o, c, 0.5,
                                                                ALU.is_lt))
            vsem(lambda e, b=b, ei=ei, c=c13: e.tensor_copy(
                msel[:, (b * 2 + 0) * 8 + ei:(b * 2 + 0) * 8 + ei + 1], c))
            vsem(lambda e, b=b, ei=ei, a=c12, c=c13: e.tensor_tensor(
                msel[:, (b * 2 + 1) * 8 + ei:(b * 2 + 1) * 8 + ei + 1],
                a, c, ALU.subtract))
            vsem(lambda e, lgs=lgs, ei=ei, b=b, c=c12:
                 e.tensor_scalar_mul(gcol[:, b * 8 + ei:b * 8 + ei + 1],
                                     lgs[:, ei:ei + 1], c))
    gates_done = cmv.n

    V.wait(dtr, wexp_ld)
    c15 = colp[:, 15:16]
    for b in range(B):
        for k2 in range(2):
            sl = b * 2 + k2
            for e8 in range(8):
                mc = msel[:, sl * 8 + e8:sl * 8 + e8 + 1]
                w1d = w1sel[:, sl * 128:(sl + 1) * 128]
                w2d = w2gsel[:, sl * 128:(sl + 1) * 128]
                w1s_ = w1t[:, e8 * 128:(e8 + 1) * 128]
                w2s_ = w2t[:, e8 * 128:(e8 + 1) * 128]
                b1d = b1sel[:, sl:sl + 1]
                V.emit(lambda e_, c=c15, mc=mc, b=b, e8=e8: e_.tensor_tensor(
                    c, gcol[:, b * 8 + e8:b * 8 + e8 + 1], mc, ALU.mult), inc=cmv)
                V.wait(cmv, cmv.n)
                if e8 == 0:
                    V.emit(lambda e_, d=w1d, s=w1s_, mc=mc:
                           e_.tensor_scalar_mul(d, s, mc))
                    V.emit(lambda e_, d=b1d, mc=mc:
                           e_.tensor_scalar_mul(d, b1t[:, 0:1], mc))
                    V.emit(lambda e_, d=w2d, s=w2s_, c=c15:
                           e_.tensor_scalar_mul(d, s, c), inc=cmv)
                else:
                    V.emit(lambda e_, d=w1d, s=w1s_, mc=mc:
                           e_.scalar_tensor_tensor(d, s, mc, d, ALU.mult, ALU.add))
                    V.emit(lambda e_, d=b1d, mc=mc, e8=e8:
                           e_.scalar_tensor_tensor(d, b1t[:, e8:e8 + 1], mc, d,
                                                   ALU.mult, ALU.add))
                    V.emit(lambda e_, d=w2d, s=w2s_, c=c15:
                           e_.scalar_tensor_tensor(d, s, c, d, ALU.mult, ALU.add),
                           inc=cmv)
                V.wait(cmv, cmv.n)
    for b in range(B):
        V.emit(lambda e, b=b: e.tensor_tensor(tmp8[:, :], b2t[:, :],
                                              gcol[:, b * 8:(b + 1) * 8], ALU.mult),
               inc=cmv)
        V.wait(cmv, cmv.n)
        V.emit(lambda e, b=b: e.reduce_sum(b2mix[:, b:b + 1], tmp8[:, :], AX.X),
               inc=cmv)
    selv = cmv.n

    T.wait(cmv, selv)
    for n in range(4):
        Q.wait(cmt, cmt.n)
        Q.emit(lambda e, n=n: e.dma_start(rowbuf[:, :],
                                          fw_d[None, 512 * n:512 * (n + 1)]),
               inc=dx, inc_n=16)
        T.wait(dx, dx.n)
        T.emit(lambda e, n=n: e.matmul(psW2[:, 512 * n:512 * (n + 1)],
                                       ones1[:, :], rowbuf[:, :],
                                       start=True, stop=True), inc=cmt)
    V.wait(cmt, cmt.n)
    V.emit(lambda e: e.tensor_copy(fwb[:, :], psW2[:, :]), inc=cmv)
    fwb_done = cmv.n
    T.wait(cmv, fwb_done)

    p2_free = {}
    xo_sp = {}
    for b in range(B):
        for i in range(NT):
            t0 = i * TIL
            par = i % 2
            gi = i // GT
            ti = i % GT
            gidx2 = b * NT + i
            yt = ysb[:, par * 2048:(par + 1) * 2048]
            xnt = xnb2[:, par * D:(par + 1) * D]
            h2t = h2b[:, par * D:(par + 1) * D]
            xnt5 = xnT5[:, par * 2048:(par + 1) * 2048]
            if gidx2 >= 2:
                Q.wait(dsp, p2_free[gidx2 - 2][1])
                Q.wait(cmv, p2_free[gidx2 - 2][2])
            Q.wait(dsp, y_spill[(b, gi)])
            src = s_y[b, gi, :, :, ti * 128:(ti + 1) * 128].rearrange(
                "k p t -> p k t")
            Q.emit(lambda e, d=yt.rearrange("p (k t) -> p k t", k=NK), s=src:
                   e.dma_start(d, s), inc=dx, inc_n=16)
            Q.emit(lambda e, d=xnt, s=s_xn[b, t0:t0 + TIL, :]:
                   e.dma_start(d, s[:, :]), inc=dx, inc_n=16)
            ld = dx.n
            for k in range(NK):
                Q.emit(lambda e, d=xnt5, k=k, b=b, t0=t0:
                       e.dma_start(d[:, k * 128:(k + 1) * 128],
                                   s_xnb[b, t0:t0 + TIL, k * 128:(k + 1) * 128],
                                   transpose=True), inc=dtr, inc_n=16)
            ldt = dtr.n
            T.wait(dx, ld)
            if gidx2 == 0:
                T.wait(dwop, wop_ld)
            if gidx2 >= 1:
                T.wait(cmv, p2_free[gidx2 - 1][0])
            for n in range(4):
                for k in range(NK):
                    T.emit(lambda e, n=n, k=k, yt=yt:
                           e.matmul(psW2[:, n * 512:(n + 1) * 512],
                                    yt[:, k * 128:(k + 1) * 128],
                                    wopsb[:, k * 2048 + n * 512:
                                          k * 2048 + (n + 1) * 512],
                                    start=(k == 0), stop=(k == NK - 1)),
                           inc=cmt if (n == 3 and k == NK - 1) else None)
            opmm = cmt.n
            V.wait(cmt, opmm)
            V.emit(lambda e, xnt=xnt:
                   e.tensor_tensor(xnt[:], psW2[:, :], xnt[:], ALU.add), inc=cmv)
            x2v = cmv.n
            c5 = colp[:, 10 + par:11 + par]
            SC.wait(cmv, x2v)
            SC.emit(lambda e, xnt=xnt, c=c5:
                    e.activation(junk[:, :], xnt[:], AF.Square, accum_out=c),
                    inc=cms)
            V.wait(cms, cms.n)
            V.emit(lambda e, c=c5: e.tensor_scalar(c, c, 1.0 / D, 1e-6,
                                                   ALU.mult, ALU.add), inc=cmv)
            SC.wait(cmv, cmv.n)
            SC.emit(lambda e, c=c5: e.activation(c, c, AF.Ln), inc=cms)
            SC.wait(cms, cms.n)
            SC.emit(lambda e, c=c5: e.activation(c, c, AF.Exp, scale=-0.5), inc=cms)
            V.wait(cms, cms.n)
            if gidx2 >= 2:
                V.wait(dsp, p2_free[gidx2 - 2][1])
            V.emit(lambda e, o=h2t, i_=xnt, c=c5:
                   e.scalar_tensor_tensor(o[:], i_[:], c, fwb[:, :],
                                          ALU.mult, ALU.mult), inc=cmv)
            h2v = cmv.n
            SC.wait(cmv, h2v)
            SC.emit(lambda e, s=xnt, b=b, t0=t0:
                    e.dma_start(s_x2[b, t0:t0 + TIL, :], s[:]), inc=dsp, inc_n=16)
            SC.emit(lambda e, s=h2t, b=b, t0=t0:
                    e.dma_start(s_h2[b, t0:t0 + TIL, :], s[:]), inc=dsp, inc_n=16)
            sp2 = dsp.n
            T.wait(dtr, ldt)
            if gidx2 >= 1:
                T.wait(cmv, p2_free[gidx2 - 1][2])
            gelu_cms = {}
            for hg in range(4):
                for ks in range(2):
                    s8 = 2 * hg + ks
                    j = s8 if s8 < 4 else s8 - 4
                    if s8 >= 4:
                        T.wait(cms, gelu_cms[j])
                    T.emit(lambda e, j=j, b=b, ks=ks, hg=hg, xnt5=xnt5:
                           e.matmul(psE[j][:, :],
                                    w1sel[:, (b * 2 + ks) * 128:
                                          (b * 2 + ks + 1) * 128],
                                    xnt5[:, hg * 512:(hg + 1) * 512],
                                    start=True, stop=True), inc=cmt)
                    SC.wait(cmt, cmt.n)
                    SC.emit(lambda e, s8=s8, j=j, b=b, ks=ks:
                            e.activation(Hsb[:, s8 * 512:(s8 + 1) * 512],
                                         psE[j][:, :], AF.Gelu,
                                         bias=b1sel[:, b * 2 + ks:b * 2 + ks + 1]),
                            inc=cms)
                    gelu_cms[s8] = cms.n
            mm2c = {}
            for hg in range(4):
                T.wait(cms, gelu_cms[max(2 * hg + 1, hg + 4)])
                for ks in range(2):
                    T.emit(lambda e, hg=hg, ks=ks, b=b:
                           e.matmul(psE[hg][:, :],
                                    w2gsel[:, (b * 2 + ks) * 128:
                                           (b * 2 + ks + 1) * 128],
                                    Hsb[:, (2 * hg + ks) * 512:
                                        (2 * hg + ks + 1) * 512],
                                    start=(ks == 0), stop=(ks == 1)),
                           inc=cmt if ks == 1 else None)
                mm2c[hg] = cmt.n
            if gidx2 >= 1:
                V.wait(dsp, xo_sp[gidx2 - 1])
            for hg in range(4):
                V.wait(cmt, mm2c[hg])
                V.emit(lambda e, hg=hg, b=b, xnt5=xnt5:
                       e.scalar_tensor_tensor(xosb[:, hg * 512:(hg + 1) * 512],
                                              psE[hg][:, :], b2mix[:, b:b + 1],
                                              xnt5[:, hg * 512:(hg + 1) * 512],
                                              ALU.add, ALU.add),
                       inc=cmv if hg == 3 else None)
            xoev = cmv.n
            SC.wait(cmv, xoev)
            SC.emit(lambda e, b=b, i=i: e.dma_start(s_xo[b, i], xosb[:, :]),
                    inc=dsp, inc_n=16)
            xo_sp[gidx2] = dsp.n
            p2_free[gidx2] = (x2v, sp2, xoev)
            drain_p0w(8)

    p2_stack.close()
    p2cmt, p2cmv, p2cms, p2dsp = cmt.n, cmv.n, cms.n, dsp.n

    p3_stack = contextlib.ExitStack()
    sb3 = lambda name, shape, dt: p3_stack.enter_context(
        nc.sbuf_tensor(name, shape, dt))
    h2Tg = sb3("h2Tg", [128, 2 * NK * GTOK], BF16)
    wgsb = sb3("wgsb", [128, 2 * NK * 128], BF16)
    wusb = sb3("wusb", [128, 2 * NK * 128], BF16)
    zgsb = sb3("zgsb", [128, 2 * GTOK], F32)
    zbsb = sb3("zbsb", [128, 2 * GTOK], BF16)
    psG = [p3_stack.enter_context(nc.psum_tensor(f"psG{j}", [128, GTOK], F32))
           for j in range(2)]
    psU = [p3_stack.enter_context(nc.psum_tensor(f"psU{j}", [128, GTOK], F32))
           for j in range(2)]

    G.wait(cmt, p2cmt)
    G.wait(cmv, p2cmv)
    Q.wait(cmt, p2cmt)
    Q.wait(cmv, p2cmv)
    Q.wait(dsp, p2dsp)

    p3_mp_done = {}
    p3_mp_free = {}
    p3_groups = [(b, gi) for b in range(B) for gi in range(NG)]
    h2T_done3 = {}
    zsp = {}

    p3_tl_done = {}

    def p3_load(g3, mp):
        b_, gi_ = p3_groups[g3]
        gmp = g3 * NF + mp
        par = mp % 2
        if gmp >= 2:
            Q.wait(cmt, p3_mp_done[divmod(gmp - 2, NF)])
        if g3 == 0:
            Q.wait(dps[0], 16 * ((len(p0w_units) - N_AIV_U) // 2))
            Q.wait(dps[1], 16 * ((len(p0w_units) - N_AIV_U) // 2))
        wg_dst = wgsb[:, par * NK * 128:(par + 1) * NK * 128]
        wu_dst = wusb[:, par * NK * 128:(par + 1) * NK * 128]
        Q.emit(lambda e, d=wg_dst, s=s_wt_g[:, mp * 2048:(mp + 1) * 2048]:
               e.dma_start(d, s), inc=dtl3, inc_n=16)
        Q.emit(lambda e, d=wu_dst, s=s_wt_u[:, mp * 2048:(mp + 1) * 2048]:
               e.dma_start(d, s), inc=dtl3, inc_n=16)
        p3_tl_done[(g3, mp)] = dtl3.n

    for g3, (b, gi) in enumerate(p3_groups):
        gbuf = g3 % 2
        if g3 >= 2:
            Q.wait(cmt, p3_mp_done[(g3 - 2, NF - 1)])
        for k in range(NK):
            dst = h2Tg[:, gbuf * NK * GTOK + k * GTOK:
                       gbuf * NK * GTOK + (k + 1) * GTOK]
            src = s_h2[b, gi * GTOK:(gi + 1) * GTOK, k * 128:(k + 1) * 128]
            Q.emit(lambda e, d=dst, s=src: e.dma_start(d, s, transpose=True),
                   inc=dtr, inc_n=16)
        h2T_done3[g3] = dtr.n
        p3_load(g3, 0)
        p3_load(g3, 1)
        for mp in range(NF):
            par = mp % 2
            gmp = g3 * NF + mp
            wg_dst = wgsb[:, par * NK * 128:(par + 1) * NK * 128]
            wu_dst = wusb[:, par * NK * 128:(par + 1) * NK * 128]
            T.wait(dtl3, p3_tl_done[(g3, mp)])
            if mp == 0:
                T.wait(dtr, h2T_done3[g3])
            if gmp >= 2:
                pg3, pmp = divmod(gmp - 2, NF)
                T.wait(cms, p3_mp_free[(pg3, pmp)][0])
                T.wait(cmv, p3_mp_free[(pg3, pmp)][1])
            for k in range(NK):
                for hh in range(2):
                    rhs = h2Tg[:, gbuf * NK * GTOK + k * GTOK + hh * 512:
                               gbuf * NK * GTOK + k * GTOK + hh * 512 + 512]
                    T.emit(lambda e, ps=psG[par], w=wg_dst[:, k * 128:(k + 1) * 128],
                           r=rhs, hh=hh, k=k:
                           e.matmul(ps[:, hh * 512:hh * 512 + 512], w, r,
                                    start=(k == 0), stop=(k == NK - 1)),
                           inc=cmt if (k == NK - 1 and hh == 1) else None)
            gate_mm = cmt.n
            for k in range(NK):
                for hh in range(2):
                    rhs = h2Tg[:, gbuf * NK * GTOK + k * GTOK + hh * 512:
                               gbuf * NK * GTOK + k * GTOK + hh * 512 + 512]
                    T.emit(lambda e, ps=psU[par], w=wu_dst[:, k * 128:(k + 1) * 128],
                           r=rhs, hh=hh, k=k:
                           e.matmul(ps[:, hh * 512:hh * 512 + 512], w, r,
                                    start=(k == 0), stop=(k == NK - 1)),
                           inc=cmt if (k == NK - 1 and hh == 1) else None)
            p3_mp_done[(g3, mp)] = cmt.n
            zg = zgsb[:, par * GTOK:(par + 1) * GTOK]
            zb = zbsb[:, par * GTOK:(par + 1) * GTOK]
            SC.wait(cmt, gate_mm)
            SC.emit(lambda e, o=zg, i_=psG[par]: e.activation(o[:], i_[:, :], AF.Silu),
                    inc=cms)
            silu_v = cms.n
            V.wait(cmt, p3_mp_done[(g3, mp)])
            V.wait(cms, silu_v)
            if gmp >= 2:
                V.wait(dsp, zsp[divmod(gmp - 2, NF)])
            V.emit(lambda e, o=zb, a=zg, bb=psU[par]:
                   e.tensor_tensor(o[:], a[:], bb[:, :], ALU.mult), inc=cmv)
            mult_v = cmv.n
            p3_mp_free[(g3, mp)] = (silu_v, mult_v)
            SC.wait(cmv, mult_v)
            SC.emit(lambda e, s=zb, b=b, gi=gi, mp=mp:
                    e.dma_start(s_z[b, gi, mp], s[:]), inc=dsp, inc_n=16)
            zsp[(g3, mp)] = dsp.n
            if mp + 2 < NF:
                p3_load(g3, mp + 2)
            drain_p0w(1)

    p3_stack.close()
    p3cmt, p3cmv, p3cms, p3dsp = cmt.n, cmv.n, cms.n, dsp.n

    p4_stack = contextlib.ExitStack()
    sb4 = lambda name, shape, dt: p4_stack.enter_context(
        nc.sbuf_tensor(name, shape, dt))
    HFW = 32 * 512
    wdnsb = sb4("wdnsb", [128, 3 * HFW], BF16)
    zsb4 = sb4("zsb4", [128, 2 * NF * 128], BF16)
    w3q = sb4("w3q", [128, NK * 512], BF16)
    xo4 = sb4("xo4", [128, 2 * 2048], BF16)
    x2h = sb4("x2h", [128, 2 * 512], F32)
    ssb4 = sb4("ssb4", [128, 2 * 512], F32)
    xq4 = sb4("xq4", [128, 2 * 512], F32)
    gq4 = sb4("gq4", [128, 2 * 512], F32)
    oq4 = sb4("oq4", [128, 2 * 512], F32)
    b3b = sb4("b3b", [128, D], F32)
    psD = [p4_stack.enter_context(nc.psum_tensor(f"psD{j}", [128, 512], F32))
           for j in range(2)]
    psW34 = [p4_stack.enter_context(nc.psum_tensor(f"psW34{j}", [128, 512], F32))
             for j in range(2)]

    G.wait(cmt, p3cmt)
    G.wait(cmv, p3cmv)
    Q.wait(cmt, p3cmt)
    Q.wait(cmv, p3cmv)
    Q.wait(dsp, p3dsp)

    T.wait(cmv, p3cmv)
    T.wait(cms, p3cms)
    for n in range(4):
        Q.wait(cmt, cmt.n)
        Q.emit(lambda e, n=n: e.dma_start(rowbuf[:, :],
                                          b3_d[None, 512 * n:512 * (n + 1)]),
               inc=dx, inc_n=16)
        T.wait(dx, dx.n)
        if n > 0:
            T.wait(cmv, cmv.n)
        T.emit(lambda e, n=n: e.matmul(psD[0][:, :], ones1[:, :], rowbuf[:, :],
                                       start=True, stop=True), inc=cmt)
        V.wait(cmt, cmt.n)
        V.emit(lambda e, n=n: e.tensor_copy(b3b[:, 512 * n:512 * (n + 1)],
                                            psD[0][:, :]), inc=cmv)
    b3_done = cmv.n
    T.wait(cmv, b3_done)

    p4_free = {}
    q_done = {}
    half_ld = {}
    for q in range(4):
        for hf in range(2):
            jh = 2 * q + hf
            slot = jh % 3
            if jh >= 3:
                Q.wait(cmt, q_done[(jh - 3) // 2])
            if q == 0:
                Q.wait(dps[0], 16 * ((len(p0w_units) - N_AIV_U) // 2))
                Q.wait(dps[1], 16 * ((len(p0w_units) - N_AIV_U) // 2))
            for kb in range(2):
                Q.emit(lambda e, d=wdnsb[:, slot * HFW + kb * 8192:
                                         slot * HFW + (kb + 1) * 8192],
                       s=s_wt_dn[q][:, hf * HFW + kb * 8192:
                                    hf * HFW + (kb + 1) * 8192]:
                       e.dma_start(d, s), inc=dtl4, inc_n=16)
            half_ld[jh] = dtl4.n
        if q >= 1:
            Q.wait(cmt, q_done[q - 1])
        for k in range(NK):
            Q.emit(lambda e, k=k, q=q: e.dma_start(
                w3q[:, k * 512:(k + 1) * 512],
                s_w3b[q * 512:(q + 1) * 512, k * 128:(k + 1) * 128],
                transpose=True), inc=dtr, inc_n=16)
        w3_ld = dtr.n
        for b in range(B):
            for i in range(NT):
                t0 = i * TIL
                par = i % 2
                gi = i // GT
                ti = i % GT
                gidx4 = q * 2 * NT + b * NT + i
                zt = zsb4[:, par * NF * 128:(par + 1) * NF * 128]
                x2t = x2h[:, par * 512:(par + 1) * 512]
                st_ = ssb4[:, par * 512:(par + 1) * 512]
                xqt = xq4[:, par * 512:(par + 1) * 512]
                gqt = gq4[:, par * 512:(par + 1) * 512]
                oqt = oq4[:, par * 512:(par + 1) * 512]
                xot = xo4[:, par * 2048:(par + 1) * 2048]
                if gidx4 >= 2:
                    Q.wait(cmv, p4_free[gidx4 - 2][1])
                    Q.wait(dout, p4_free[gidx4 - 2][2])
                src = s_z[b, gi, :, :, ti * 128:(ti + 1) * 128].rearrange(
                    "f p t -> p f t")
                Q.emit(lambda e, d=zt.rearrange("p (f t) -> p f t", f=NF), s=src:
                       e.dma_start(d, s), inc=dx, inc_n=16)
                Q.emit(lambda e, d=x2t, s=s_x2[b, t0:t0 + TIL,
                                               q * 512:(q + 1) * 512]:
                       e.dma_start(d, s[:, :]), inc=dx, inc_n=16)
                Q.emit(lambda e, d=xqt, s=x_d[b, t0:t0 + TIL,
                                              q * 512:(q + 1) * 512]:
                       e.dma_start(d, s[:, :]), inc=dx, inc_n=16)
                Q.emit(lambda e, d=xot, s=s_xo[b, i]:
                       e.dma_start(d, s[:, :]), inc=dx, inc_n=16)
                ld4 = dx.n
                T.wait(dx, ld4)
                if gidx4 % (2 * NT) == 0:
                    T.wait(dtl4, half_ld[2 * q + 1])
                    T.wait(dtr, w3_ld)
                if gidx4 >= 2:
                    T.wait(cmv, p4_free[gidx4 - 2][1])
                ps = psD[par]
                for f in range(NF):
                    slot = (2 * q + f // 32) % 3
                    T.emit(lambda e, ps=ps, f=f, zt=zt, slot=slot:
                           e.matmul(ps[:, :], zt[:, f * 128:(f + 1) * 128],
                                    wdnsb[:, slot * HFW + (f % 32) * 512:
                                          slot * HFW + (f % 32 + 1) * 512],
                                    start=(f == 0), stop=(f == NF - 1)),
                           inc=cmt if f == NF - 1 else None)
                dnc = cmt.n
                pw3 = psW34[par]
                for k in range(NK):
                    T.emit(lambda e, pw3=pw3, k=k, xot=xot:
                           e.matmul(pw3[:, :], xot[:, k * 128:(k + 1) * 128],
                                    w3q[:, k * 512:(k + 1) * 512],
                                    start=(k == 0), stop=(k == NK - 1)),
                           inc=cmt if k == NK - 1 else None)
                w3c = cmt.n
                V.wait(cmt, dnc)
                V.emit(lambda e, o=st_, a=ps, bb=x2t:
                       e.tensor_tensor(o[:], a[:, :], bb[:], ALU.add), inc=cmv)
                sv = cmv.n
                V.wait(cmt, w3c)
                V.emit(lambda e, o=gqt, a=pw3, q=q:
                       e.scalar_tensor_tensor(o[:], a[:, :], 1.0,
                                              b3b[:, q * 512:(q + 1) * 512],
                                              ALU.mult, ALU.add), inc=cmv)
                gv = cmv.n
                SC.wait(cmv, gv)
                SC.emit(lambda e, o=gqt: e.activation(o[:], o[:], AF.Silu),
                        inc=cms)
                gsil = cms.n
                V.wait(cms, gsil)
                V.emit(lambda e, o=gqt, s=st_:
                       e.tensor_tensor(o[:], o[:], s[:], ALU.mult))
                V.emit(lambda e, o=oqt, a=gqt, x=xqt:
                       e.tensor_tensor(o[:], a[:], x[:], ALU.add), inc=cmv)
                ov = cmv.n
                SC.wait(cmv, ov)
                SC.emit(lambda e, s=oqt, b=b, t0=t0, q=q:
                        e.dma_start(out_d[b, t0:t0 + TIL,
                                          q * 512:(q + 1) * 512], s[:]),
                        inc=dout, inc_n=16)
                p4_free[gidx4] = (sv, ov, dout.n)
        q_done[q] = cmt.n

    Q.wait(dout, dout.n)

    p4_stack.close()
    return dict(nc=nc, stack=stack, streams=streams)


def finalize(ctx):
    nc = ctx["nc"]
    T, V, SC, Q, G = ctx["streams"]
    check_deadlock(ctx["streams"])
    with nc.Block() as block:
        @block.tensor
        def _(t):
            T.play(t)

        @block.vector
        def _(v):
            V.play(v)

        @block.scalar
        def _(s):
            SC.play(s)

        @block.sync
        def _(q):
            Q.play(q)

        @block.gpsimd
        def _(g):
            G.play(g)
    ctx["stack"].close()
    return nc


KEYMAP = dict(
    x="L0_attn_x", pw="L0_attn_pre_norm_w", wa="L0_attn_w_a", wi="L0_attn_w_i",
    wv="L0_attn_w_v", wop="L0_attn_out_proj_w", db="L0_attn_decay_bias",
    fw="L0_attn_ffn_norm_w", wgu="L0_attn_w_gate_up", wdn="L0_attn_w_down",
    sw="L0_ffn_switch_w", sb="L0_ffn_switch_b", w1="L0_ffn_w1", b1="L0_ffn_b1",
    w2="L0_ffn_w2", b2="L0_ffn_b2", w3="L0_ffn_w3", b3="L0_ffn_b3")


def kernel(**inputs):
    ctx = build(dbg=False)
    nc = finalize(ctx)
    full_x = np.ascontiguousarray(np.asarray(inputs["L0_attn_x"], np.float32))
    weights = {k: np.ascontiguousarray(np.asarray(inputs[v], np.float32))
               for k, v in KEYMAP.items() if k != "x"}
    weights["sw"] = weights["sw"].reshape(8, S * D)
    in_maps = []
    for c in range(8):
        m = dict(weights)
        m["x"] = full_x[2 * c:2 * c + 2]
        in_maps.append(m)
    res = run_bass_kernel_spmd(nc, in_maps, core_ids=list(range(8)))
    return np.concatenate([r["out"] for r in res.results], axis=0)
